# version 11 (final)
# baseline (speedup 1.0000x reference)
import numpy as np
import ml_dtypes
from contextlib import ExitStack

import concourse.bass as bass
import concourse.mybir as mybir
from concourse.bass_utils import run_bass_kernel_spmd

F32 = mybir.dt.float32
BF16 = mybir.dt.bfloat16
AF = mybir.ActivationFunctionType
ALU = mybir.AluOpType
AX = mybir.AxisListType

S = 2048
D = 2048
NK = 16
HEADS_A = 8
DH_A = 256
CHUNK = 256
NCH = S // CHUNK
HQ, HKV, HD, WIN = 16, 4, 128, 512
NE, NI = 16, 1024
NSUB = 4
FUSE_STATS = True
USE_WCACHE = True
DBG_ATTNB = False

BF = ml_dtypes.bfloat16


class Sched:
    ENGS = ("sync", "gpsimd", "tensor", "vector", "scalar")

    def __init__(self, nc, es):
        self.nc = nc
        self.ops = {e: [] for e in self.ENGS}
        self.sem = {}
        self.cnt = {}
        self.lastwait = {e: {} for e in self.ENGS}
        for name in ("qW", "qX", "qS", "PE", "VE", "SE"):
            self.sem[name] = es.enter_context(nc.semaphore(name))
            self.cnt[name] = 0

    def wait(self, eng, sname, val):
        if val is None or val <= 0:
            return
        lw = self.lastwait[eng]
        if lw.get(sname, 0) >= val:
            return
        lw[sname] = val
        s = self.sem[sname]
        self.ops[eng].append(lambda e, s=s, v=val: e.wait_ge(s, v))

    def waits(self, eng, deps):
        for d in deps:
            if d is not None:
                self.wait(eng, d[0], d[1])

    def run(self, eng, fn, inc=None, step=1):
        if inc is None:
            self.ops[eng].append(fn)
            return None
        self.cnt[inc] += step
        v = self.cnt[inc]
        s = self.sem[inc]
        self.ops[eng].append(lambda e, fn=fn, s=s, st=step: fn(e).then_inc(s, st))
        return (inc, v)

    QSEM = {"gpsimd": "qW", "sync": "qX", "scalar": "qS"}

    def dma(self, q, out, in_, deps=()):
        self.waits(q, deps)
        sem = self.QSEM[q]
        return self.run(q, lambda e, o=out, i=in_: e.dma_start(out=o, in_=i),
                        inc=sem, step=16)

    def finish(self):
        self.wait("sync", "qX", self.cnt["qX"])
        self.wait("gpsimd", "qW", self.cnt["qW"])
        self.wait("scalar", "qS", self.cnt["qS"])


def host_constants():
    c = {}
    c["ident_bf"] = np.eye(128, dtype=BF)
    c["ident_f32"] = np.eye(128, dtype=np.float32)
    c["ones1_f32"] = np.ones((1, 128), dtype=np.float32)
    c["ones1_bf"] = np.ones((1, 128), dtype=BF)
    c["ones128_bf"] = np.ones((128, 1), dtype=BF)

    betas = np.linspace(np.log2(1.0 / 32.0), np.log2(1.0 / 512.0), HEADS_A)
    g = (1.0 - np.exp2(betas)).astype(np.float64)
    idx = np.arange(CHUNK)
    i_, j_ = idx[:, None], idx[None, :]
    DT = np.zeros((128, HEADS_A * 2 * CHUNK), dtype=np.float32)
    for h in range(HEADS_A):
        Dh = np.where(i_ >= j_, g[h] ** (i_ - j_).astype(np.float64), 0.0)
        DhT = Dh.T.astype(np.float32)
        for jt in range(2):
            DT[:, (h * 2 + jt) * CHUNK:(h * 2 + jt + 1) * CHUNK] = DhT[jt * 128:(jt + 1) * 128, :]
    c["DT"] = DT.astype(BF)
    cs = np.zeros((128, HEADS_A * CHUNK), dtype=np.float32)
    kv = np.zeros((128, HEADS_A * CHUNK), dtype=np.float32)
    for h in range(HEADS_A):
        cs[:, h * CHUNK:(h + 1) * CHUNK] = (g[h] ** (idx + 1).astype(np.float64))[None, :]
        kv[:, h * CHUNK:(h + 1) * CHUNK] = (g[h] ** (CHUNK - 1 - idx).astype(np.float64))[None, :]
    c["cs_bc"] = cs.astype(BF)
    c["kv_bc"] = kv.astype(BF)
    c["chunk_decay"] = (g ** CHUNK).astype(np.float64)

    k_, q_ = np.arange(128)[:, None], np.arange(128)[None, :]
    c["tri1"] = (k_ <= q_).astype(BF)
    c["tri2"] = (k_ >= q_ + 1).astype(BF)
    return c


def relayout_weights(inp):
    w = {}
    w["Wqa"] = np.ascontiguousarray(
        inp["L0_attn_a_Wq"].reshape(NK, 128, NK, 128).transpose(2, 0, 1, 3))
    w["Wka"] = np.ascontiguousarray(
        inp["L0_attn_a_Wk"].reshape(NK, 128, NK, 128).transpose(2, 0, 1, 3))
    w["Wva"] = np.ascontiguousarray(
        inp["L0_attn_a_Wv"].reshape(NK, 128, 4, 512).transpose(2, 0, 1, 3))
    w["bqa"] = np.ascontiguousarray(inp["L0_attn_a_bq"].reshape(NK, 128).T)
    w["bka"] = np.ascontiguousarray(inp["L0_attn_a_bk"].reshape(NK, 128).T)
    w["bva"] = np.ascontiguousarray(inp["L0_attn_a_bv"].reshape(1, D))

    w["Wqb"] = np.ascontiguousarray(
        inp["L0_attn_b_Wq"].reshape(NK, 128, NK, 128).transpose(2, 0, 1, 3))
    w["Wkb"] = np.ascontiguousarray(
        inp["L0_attn_b_Wk"].reshape(NK, 128, 4, 128).transpose(2, 0, 1, 3))
    w["Wvb"] = np.ascontiguousarray(inp["L0_attn_b_Wv"].reshape(NK, 128, 512))
    w["Wob"] = np.ascontiguousarray(
        inp["L0_attn_b_Wo"].reshape(NK, 128, 4, 512).transpose(2, 0, 1, 3))

    gup = inp["L0_ffn_gate_up_proj"]
    gg = gup[:, :, :NI].reshape(NE, NK, 128, 8, 128).transpose(0, 3, 1, 2, 4)
    uu = gup[:, :, NI:].reshape(NE, NK, 128, 8, 128).transpose(0, 3, 1, 2, 4)
    w["Wgu"] = np.ascontiguousarray(np.concatenate([gg, uu], axis=-1))
    w["Wdn"] = np.ascontiguousarray(
        inp["L0_ffn_down_proj"].reshape(NE, 8, 128, 2, 1024).transpose(0, 3, 1, 2, 4))
    w["gwT"] = np.ascontiguousarray(inp["L0_ffn_gate_w"].T.reshape(NK, 128, NE))
    return w


def build_program(nsub=NSUB):
    global DBG_ATTNB, FUSE_STATS, USE_WCACHE
    nc = bass.Bass("TRN2", target_bir_lowering=False)
    dr = {}

    def din(name, shape, dtype=F32):
        dr[name] = nc.dram_tensor(name, shape, dtype, kind="ExternalInput").ap()
        return dr[name]

    x_in = din("x", [S, D])
    din("Wqa", [NK, NK, 128, 128]); din("Wka", [NK, NK, 128, 128])
    din("Wva", [4, NK, 128, 512])
    din("bqa", [128, NK]); din("bka", [128, NK]); din("bva", [1, D])
    din("Wqb", [NK, NK, 128, 128]); din("Wkb", [4, NK, 128, 128])
    din("Wvb", [NK, 128, 512]); din("Wob", [4, NK, 128, 512])
    din("Wgu", [NE, 8, NK, 128, 256]); din("Wdn", [NE, 2, 8, 128, 1024])
    din("gwT", [NK, 128, NE])
    din("ident_bf", [128, 128], BF16)
    din("ones1_f32", [1, 128]); din("ones128_bf", [128, 1], BF16)
    din("ones1_bf", [1, 128], BF16)
    din("DT", [128, HEADS_A * 2 * CHUNK], BF16)
    din("cs_bc", [128, HEADS_A * CHUNK], BF16); din("kv_bc", [128, HEADS_A * CHUNK], BF16)
    din("tri1", [128, 128], BF16); din("tri2", [128, 128], BF16)

    x_cur = nc.dram_tensor("x_cur", [S, D], F32, kind="Internal").ap()
    wqa_c = nc.dram_tensor("wqa_c", [NK, 128, NK * 128], BF16, kind="Internal").ap()
    wka_c = nc.dram_tensor("wka_c", [NK, 128, NK * 128], BF16, kind="Internal").ap()
    wva_c = nc.dram_tensor("wva_c", [4, 128, NK * 512], BF16, kind="Internal").ap()
    wgu_c = nc.dram_tensor("wgu_c", [NE, 8, 128, NK * 256], BF16, kind="Internal").ap()
    wqb_c = nc.dram_tensor("wqb_c", [HQ, 128, NK * 128], BF16, kind="Internal").ap()
    wkb_c = nc.dram_tensor("wkb_c", [4, 128, NK * 128], BF16, kind="Internal").ap()
    wvb_c = nc.dram_tensor("wvb_c", [8, 128, 2 * 512], BF16, kind="Internal").ap()
    wob_c = nc.dram_tensor("wob_c", [4, 128, NK * 512], BF16, kind="Internal").ap()
    wdn_c = nc.dram_tensor("wdn_c", [NE, 2, 128, 8 * 1024], BF16, kind="Internal").ap()
    dbg = {}
    if DBG_ATTNB:
        dbg["attnT"] = nc.dram_tensor("dbg_attnT", [128, NK * 1024], BF16, kind="ExternalOutput").ap()
        dbg["negm"] = nc.dram_tensor("dbg_negm", [1, 1024], BF16, kind="ExternalOutput").ap()
        dbg["kTb"] = nc.dram_tensor("dbg_kTb", [128, 4 * S], BF16, kind="ExternalOutput").ap()
        dbg["vb"] = nc.dram_tensor("dbg_vb", [128, NK * 512], BF16, kind="ExternalOutput").ap()
        dbg["qTh"] = nc.dram_tensor("dbg_qTh", [128, 1024], BF16, kind="ExternalOutput").ap()
    y_out = nc.dram_tensor("y", [S, D], F32, kind="ExternalOutput").ap()

    consts = host_constants()
    cdec = consts["chunk_decay"]

    es = ExitStack()
    with es:
        sch = Sched(nc, es)

        sb = lambda n, sh, dt=F32: es.enter_context(nc.sbuf_tensor(n, sh, dt))
        ident_bf = sb("identbf_sb", [128, 128], BF16)
        ones1 = sb("ones1_sb", [1, 128], F32)
        ones1b = sb("ones1b_sb", [1, 128], BF16)
        ones128 = sb("ones128_sb", [128, 1], BF16)
        gwT_sb = sb("gwT_sb", [128, NK * NE], BF16)
        bnst = sb("bnst_sb", [128, 24], F32)
        bnst2 = sb("bnst2_sb", [128, 24], F32)
        bnst3 = sb("bnst3_sb", [128, 8 * 24], F32)
        mvst = sb("mv_sb", [128, 2 * NK], F32)
        sd = sb("sd_sb", [128, NK], F32); rstd = sb("rstd_sb", [128, NK], F32)
        nb = sb("nb_sb", [128, NK], F32)
        xt = [sb(f"xt{i}", [128, D], F32) for i in range(2)]
        lntm = [sb(f"lntm{i}", [128, D], BF16) for i in range(2)]

        ps = lambda n, sh, dt=F32: es.enter_context(nc.psum_tensor(n, sh, dt))
        psf = [ps(f"psf{i}", [128, 512], F32) for i in range(4)]
        psb2 = ps("psb2", [128, 1024], F32)
        psbf = [ps(f"psbf{i}", [128, 512], BF16) for i in range(1)]
        pssm = ps("pssm", [1, 512], F32)

        cl = []
        for name, t in (("ident_bf", ident_bf), ("ones1_bf", ones1b),
                        ("ones1_f32", ones1), ("ones128_bf", ones128)):
            cl.append(sch.dma("sync", t[:], dr[name]))
        ev_gw = sch.dma("gpsimd", gwT_sb.ap().rearrange("p (k e) -> p k e", k=NK),
                        dr["gwT"])
        ev_consts = cl[-1]
        for eng in ("tensor", "vector", "scalar"):
            sch.wait(eng, "qX", ev_consts[1])

        last_x_store = None
        xt_last = {0: [], 1: []}
        stats_ready = {}
        psum_free = {id(p): None for p in psf + [psb2] + psbf + [pssm]}

        def pswait(eng, p):
            d = psum_free[id(p)]
            if d is not None:
                sch.waits(eng, [d])

        def psfree(p, mark):
            psum_free[id(p)] = mark

        def emit_ln(src, gis, lnT, lnT_stride, toff_fn, dep0=None, stats_marks=None):
            n = len(gis)
            sch.wait("vector", "SE", sch.cnt["SE"])
            sch.wait("scalar", "VE", sch.cnt["VE"])
            war = {}
            last_aggr = None
            for t, gi in enumerate(gis) if stats_marks is None else ():
                deps = [dep0] if dep0 else []
                if t >= 2:
                    deps += war[t - 2]
                else:
                    deps += xt_last[t % 2]
                ev = sch.dma("sync", xt[t % 2][:], src[gi * 128:(gi + 1) * 128, :], deps)
                sch.wait("vector", "qX", ev[1])
                mb = None
                for i4 in range(4):
                    mb = sch.run("vector", lambda e, i4=i4, b=xt[t % 2]:
                                 e.bn_stats(bnst[:, i4 * 6:(i4 + 1) * 6],
                                            b[:, i4 * 512:(i4 + 1) * 512]),
                                 inc="VE" if i4 == 3 else None)
                sch.wait("vector", "VE", mb[1])
                ma = sch.run("vector", lambda e, gi=gi:
                             e.bn_aggr(mvst[:, 2 * gi:2 * gi + 2], bnst[:, 0:24]), inc="VE")
                war[t] = [ma]
                xt_last[t % 2] = [ma]
                last_aggr = ma
            c0 = gis[0]
            mvv = mvst.ap().rearrange("p (g c) -> p g c", c=2)
            mean_v = mvv[:, c0:c0 + n, 0:1]
            var_v = mvv[:, c0:c0 + n, 1:2]
            sl = lambda b: b[:, c0:c0 + n]
            if stats_marks is None:
                sch.wait("vector", "VE", last_aggr[1])
            else:
                sch.wait("vector", "VE", max(stats_marks[gi][1] for gi in gis))
            m4 = sch.run("vector", lambda e: e.tensor_scalar(sl(sd), var_v, 1e-5, None, ALU.add), inc="VE")
            sch.wait("scalar", "VE", m4[1])
            m5 = sch.run("scalar", lambda e: e.activation(sl(sd), sl(sd), AF.Sqrt), inc="SE")
            sch.wait("vector", "SE", m5[1])
            m6 = sch.run("vector", lambda e: e.reciprocal(sl(rstd), sl(sd)), inc="VE")
            sch.wait("vector", "VE", m6[1])
            m7 = sch.run("vector", lambda e: e.scalar_tensor_tensor(
                sl(nb), mean_v, -1.0, sl(rstd), ALU.mult, ALU.mult), inc="VE")
            sch.wait("scalar", "VE", m7[1])
            norm_marks = {}
            evac_marks = {}
            war2 = {}
            tr_last = [None, None]
            lv = lnT.ap().rearrange("p (k t) -> p k t", t=lnT_stride)
            last_ev = None
            for t, gi in enumerate(gis):
                if t >= 2:
                    deps = war2[t - 2]
                else:
                    deps = xt_last[t % 2]
                ev = sch.dma("sync", xt[t % 2][:], src[gi * 128:(gi + 1) * 128, :], deps)
                last_ev = ev
                sch.wait("scalar", "qX", ev[1])
                if tr_last[t % 2] is not None:
                    sch.wait("scalar", "PE", tr_last[t % 2][1])
                mn = sch.run("scalar", lambda e, o=lntm[t % 2], i=xt[t % 2], gi=gi:
                             e.activation(o[:], i[:], AF.Identity,
                                          bias=nb[:, gi:gi + 1], scale=rstd[:, gi:gi + 1]),
                             inc="SE")
                norm_marks[gi] = mn
                toff = toff_fn(gi)
                gmarks = []
                mt = None
                for gblk in range(4):
                    p = psbf[0]
                    sch.wait("tensor", "SE", mn[1])
                    pswait("tensor", p)
                    for j in range(4):
                        k = gblk * 4 + j
                        mt = sch.run("tensor", lambda e, p=p, j=j, k=k, src_t=lntm[t % 2]:
                                     e.transpose(p[:, j * 128:(j + 1) * 128],
                                                 src_t[:, k * 128:(k + 1) * 128], ident_bf[:]),
                                     inc="PE" if j == 3 else None)
                    sch.wait("vector", "PE", mt[1])
                    me = sch.run("vector", lambda e, p=p, gblk=gblk, toff=toff:
                                 e.tensor_copy(lv[:, gblk * 4:gblk * 4 + 4, toff:toff + 128],
                                               p.ap().rearrange("p (k t) -> p k t", t=128)),
                                 inc="VE")
                    psfree(p, me)
                    gmarks.append(me)
                tr_last[t % 2] = mt
                evac_marks[gi] = gmarks[-1]
                war2[t] = [norm_marks[gi], gmarks[-1], ev]
                xt_last[t % 2] = [norm_marks[gi]]
            return norm_marks, evac_marks, last_ev

        def emit_attn_a():
            nonlocal last_x_store
            ph = ExitStack()
            with ph:
                lnTq = ph.enter_context(nc.sbuf_tensor("lnTq", [128, NK * 512], BF16))
                qT = ph.enter_context(nc.sbuf_tensor("qTa", [128, NK * 512], BF16))
                kT = ph.enter_context(nc.sbuf_tensor("kTa", [128, NK * 512], BF16))
                vtm = ph.enter_context(nc.sbuf_tensor("vta", [128, 4 * D], BF16))
                S32 = ph.enter_context(nc.sbuf_tensor("S32", [128, HEADS_A * 512], F32))
                Sbf = ph.enter_context(nc.sbuf_tensor("Sbf", [128, HEADS_A * 512], BF16))
                wcb = [ph.enter_context(nc.sbuf_tensor(f"wcba{i}", [128, NK * 128], BF16))
                       for i in range(2)]
                wvcb = [ph.enter_context(nc.sbuf_tensor(f"wvcba{i}", [128, NK * 512], BF16))
                        for i in range(2)]
                xc = [ph.enter_context(nc.sbuf_tensor(f"xca{i}", [128, D], F32)) for i in range(2)]
                qcs = [ph.enter_context(nc.sbuf_tensor(f"qcsa{i}", [128, 512], BF16)) for i in range(2)]
                DT_sb = ph.enter_context(nc.sbuf_tensor("DT_sb", [128, HEADS_A * 2 * CHUNK], BF16))
                cs_sb = ph.enter_context(nc.sbuf_tensor("cs_sb", [128, HEADS_A * CHUNK], BF16))
                kv_sb = ph.enter_context(nc.sbuf_tensor("kv_sb", [128, HEADS_A * CHUNK], BF16))
                bqa_sb = ph.enter_context(nc.sbuf_tensor("bqa_sb", [128, NK], F32))
                bka_sb = ph.enter_context(nc.sbuf_tensor("bka_sb", [128, NK], F32))
                bva_sb = ph.enter_context(nc.sbuf_tensor("bva_sb", [1, D], BF16))
                bva_bc = ph.enter_context(nc.sbuf_tensor("bva_bc", [128, D], BF16))
                pc = []
                for name, t in (("DT", DT_sb), ("cs_bc", cs_sb), ("kv_bc", kv_sb),
                                ("bqa", bqa_sb), ("bka", bka_sb)):
                    pc.append(sch.dma("sync", t[:], dr[name]))
                pv = sch.dma("gpsimd", bva_sb[:], dr["bva"])
                for eng in ("tensor", "vector", "scalar"):
                    sch.wait(eng, "qX", pc[-1][1])
                sch.wait("tensor", "qW", pv[1])
                for jq in range(4):
                    p = psf[jq % 2]
                    pswait("tensor", p)
                    mmb = sch.run("tensor", lambda e, p=p, jq=jq:
                                  e.matmul(p[:], ones1b[:], bva_sb[:, jq * 512:(jq + 1) * 512],
                                           start=True, stop=True), inc="PE")
                    sch.wait("vector", "PE", mmb[1])
                    meb = sch.run("vector", lambda e, p=p, jq=jq:
                                  e.tensor_copy(bva_bc[:, jq * 512:(jq + 1) * 512], p[:]), inc="VE")
                    psfree(p, meb)
                bva_bc_done = ("VE", sch.cnt["VE"])
                sch.wait("vector", "VE", bva_bc_done[1])
                ksc = [ph.enter_context(nc.sbuf_tensor(f"ksca{i}", [128, 512], BF16)) for i in range(2)]
                ktr = [ph.enter_context(nc.sbuf_tensor(f"ktra{i}", [128, 512], BF16)) for i in range(2)]
                atc = [ph.enter_context(nc.sbuf_tensor(f"atca{i}", [128, 512], BF16)) for i in range(2)]

                mS0 = sch.run("vector", lambda e: e.memset(S32[:], 0.0), inc="VE")
                S_copy_mark = {h: mS0 for h in range(HEADS_A)}
                S_fma_mark = {h: mS0 for h in range(HEADS_A)}
                psc2 = [psf[0], psb2]
                pso2 = [psf[1], psb2]
                psS2 = [psf[2], psf[3]]
                prev_out_mm = [None, None]
                prev_ktr_mm = [None, None]
                prev_S_mm = [None, None]
                prev_at_rd = [None, None]
                prev_store = [None, None]
                chunk_xwar = [None, None]
                wq_last_use = [[], []]
                wv_last_use = [[], []]
                last_proj_mm = None

                def do_ln(qt):
                    gis = list(range(qt * 4, qt * 4 + 4))
                    _, evac, _ = emit_ln(x_in, gis, lnTq, 512,
                                         lambda gi, qt=qt: gi * 128 - qt * 512,
                                         dep0=ev_consts if qt == 0 else None)
                    return evac[gis[-1]]

                ln_next = do_ln(0)
                for qt in range(4):
                    ln_done = ln_next

                    ldix = 0
                    for mat, (wdr, wcache, wsb, bias) in enumerate(
                            (("Wqa", wqa_c, qT, bqa_sb), ("Wka", wka_c, kT, bka_sb))):
                        for jb in range(NK):
                            slot = ldix % 2
                            if qt == 0:
                                ev = sch.dma("gpsimd",
                                             wcb[slot].ap().rearrange("p (k j) -> p k j", k=NK),
                                             dr[wdr][jb].rearrange("k p j -> p k j"),
                                             deps=wq_last_use[slot])
                                sev = sch.dma("sync", wcache[jb], wcb[slot][:], deps=[ev])
                                wsem = "qW"
                            else:
                                qq = "scalar" if slot == 0 else "sync"
                                ev = sch.dma(qq, wcb[slot][:], wcache[jb],
                                             deps=wq_last_use[slot])
                                sev = None
                                wsem = sch.QSEM[qq]
                            p = psf[ldix % 3]
                            sch.wait("tensor", wsem, ev[1])
                            sch.waits("tensor", [ln_done])
                            pswait("tensor", p)
                            mm = None
                            for k in range(NK):
                                mm = sch.run("tensor", lambda e, p=p, k=k, slot=slot:
                                             e.matmul(p[:], wcb[slot][:, k * 128:(k + 1) * 128],
                                                      lnTq[:, k * 512:(k + 1) * 512],
                                                      start=(k == 0), stop=(k == NK - 1)),
                                             inc="PE" if k == NK - 1 else None)
                            wq_last_use[slot] = [mm] + ([sev] if sev else [])
                            last_proj_mm = mm
                            sch.wait("scalar", "PE", mm[1])
                            me = sch.run("scalar", lambda e, p=p, wsb=wsb, jb=jb, bias=bias:
                                         e.activation(wsb[:, jb * 512:(jb + 1) * 512], p[:],
                                                      AF.Identity, bias=bias[:, jb:jb + 1]),
                                         inc="SE")
                            psfree(p, me)
                            ldix += 1
                    qk_evac_done = ("SE", sch.cnt["SE"])

                    for jb4 in range(4):
                        slot = jb4 % 2
                        if qt == 0:
                            ev = sch.dma("gpsimd",
                                         wvcb[slot].ap().rearrange("p (k j) -> p k j", k=NK),
                                         dr["Wva"][jb4].rearrange("k p j -> p k j"),
                                         deps=wv_last_use[slot])
                            vsev = sch.dma("sync", wva_c[jb4], wvcb[slot][:], deps=[ev])
                            vwsem = "qW"
                        else:
                            qq = "scalar" if jb4 % 2 == 0 else "sync"
                            ev = sch.dma(qq, wvcb[slot][:], wva_c[jb4],
                                         deps=wv_last_use[slot])
                            vsev = None
                            vwsem = sch.QSEM[qq]
                        for tt in range(4):
                            p = psf[tt % 3] if tt < 3 else psb2
                            pap = p[:] if tt < 3 else p[:, 0:512]
                            sch.wait("tensor", vwsem, ev[1])
                            sch.waits("tensor", [ln_done])
                            pswait("tensor", p)
                            mm = None
                            for k in range(NK):
                                mm = sch.run("tensor", lambda e, pap=pap, k=k, tt=tt, slot=slot:
                                             e.matmul(pap, lnTq[:, k * 512 + tt * 128: k * 512 + tt * 128 + 128],
                                                      wvcb[slot][:, k * 512:(k + 1) * 512],
                                                      start=(k == 0), stop=(k == NK - 1)),
                                             inc="PE" if k == NK - 1 else None)
                            wv_last_use[slot] = [mm] + ([vsev] if vsev else [])
                            last_proj_mm = mm
                            sch.wait("vector", "PE", mm[1])
                            me = sch.run("vector", lambda e, pap=pap, tt=tt, jb4=jb4:
                                         e.tensor_tensor(vtm[:, tt * D + jb4 * 512: tt * D + (jb4 + 1) * 512],
                                                         pap, bva_bc[:, jb4 * 512:(jb4 + 1) * 512], ALU.add),
                                         inc="VE")
                            psfree(p, me)
                    v_evac_done = ("VE", sch.cnt["VE"])

                    if qt + 1 < 4:
                        ln_next = do_ln(qt + 1)


                    for c in range(2):
                        n = qt * 2 + c
                        xev = []
                        for it in range(2):
                            gi = qt * 4 + c * 2 + it
                            deps = [prev_store[it], chunk_xwar[it]]
                            xev.append(sch.dma("sync", xc[it][:],
                                               x_in[gi * 128:(gi + 1) * 128, :], deps))
                        out_evac_marks = []
                        for h in range(HEADS_A):
                            u = (n * HEADS_A + h) % 2
                            qb = qcs[u]; kb = ksc[u]; tb_ = ktr[u]; ab = atc[u]
                            p_sc = psc2[u]
                            p_o = pso2[u]
                            psS = psS2[u]
                            oof = 0 if u == 0 else 512
                            p_kt = psbf[0]
                            tc = c * 256
                            sch.waits("vector", [qk_evac_done, prev_out_mm[u]])
                            mq = None
                            for dt_ in range(2):
                                mq = sch.run("vector", lambda e, dt_=dt_, h=h, tc=tc, qb=qb:
                                             e.tensor_tensor(qb[:, dt_ * 256:(dt_ + 1) * 256],
                                                             qT[:, (2 * h + dt_) * 512 + tc:(2 * h + dt_) * 512 + tc + 256],
                                                             cs_sb[:, h * 256:(h + 1) * 256], ALU.mult),
                                             inc="VE")
                            sch.waits("vector", [prev_ktr_mm[u]])
                            mk = None
                            for dt_ in range(2):
                                mk = sch.run("vector", lambda e, dt_=dt_, h=h, tc=tc, kb=kb:
                                             e.tensor_tensor(kb[:, dt_ * 256:(dt_ + 1) * 256],
                                                             kT[:, (2 * h + dt_) * 512 + tc:(2 * h + dt_) * 512 + tc + 256],
                                                             kv_sb[:, h * 256:(h + 1) * 256], ALU.mult),
                                             inc="VE")
                            sch.waits("tensor", [qk_evac_done])
                            pswait("tensor", p_sc)
                            msc = None
                            for jt in range(2):
                                for dt_ in range(2):
                                    msc = sch.run("tensor", lambda e, jt=jt, dt_=dt_, h=h, tc=tc, p_sc=p_sc:
                                                  e.matmul(p_sc[:, jt * 256:(jt + 1) * 256],
                                                           kT[:, (2 * h + dt_) * 512 + tc + jt * 128:
                                                              (2 * h + dt_) * 512 + tc + jt * 128 + 128],
                                                           qT[:, (2 * h + dt_) * 512 + tc:(2 * h + dt_) * 512 + tc + 256],
                                                           start=(dt_ == 0), stop=(dt_ == 1)),
                                                  inc="PE" if (jt == 1 and dt_ == 1) else None)
                            sch.wait("tensor", "VE", mk[1])
                            pswait("tensor", p_kt)
                            mtr = None
                            for jt in range(2):
                                for dt_ in range(2):
                                    mtr = sch.run("tensor", lambda e, jt=jt, dt_=dt_, kb=kb, p_kt=p_kt:
                                                  e.transpose(p_kt[:, (jt * 2 + dt_) * 128:(jt * 2 + dt_ + 1) * 128],
                                                              kb[:, dt_ * 256 + jt * 128: dt_ * 256 + jt * 128 + 128],
                                                              ident_bf[:]),
                                                  inc="PE" if (jt == 1 and dt_ == 1) else None)
                            prev_ktr_mm[u] = mtr
                            sch.wait("vector", "PE", msc[1])
                            sch.waits("vector", [prev_at_rd[u]])
                            mdt = None
                            for jt in range(2):
                                mdt = sch.run("vector", lambda e, jt=jt, h=h, ab=ab, p_sc=p_sc:
                                              e.tensor_tensor(ab[:, jt * 256:(jt + 1) * 256],
                                                              p_sc[:, jt * 256:(jt + 1) * 256],
                                                              DT_sb[:, (h * 2 + jt) * 256:(h * 2 + jt + 1) * 256],
                                                              ALU.mult),
                                              inc="VE")
                            psfree(p_sc, mdt)
                            sch.wait("vector", "PE", mtr[1])
                            sch.waits("vector", [prev_S_mm[u]])
                            mke = sch.run("vector", lambda e, tb_=tb_, p_kt=p_kt: e.tensor_copy(tb_[:], p_kt[:]), inc="VE")
                            psfree(p_kt, mke)
                            sch.wait("tensor", "VE", mdt[1])
                            sch.waits("tensor", [v_evac_done, ("VE", mq[1]),
                                                 ("VE", S_copy_mark[h][1]) if n > 0 else None])
                            pswait("tensor", p_o)
                            mo = None
                            for it in range(2):
                                nmm = 4 if n > 0 else 2
                                ix = 0
                                for jt in range(2):
                                    mo = sch.run("tensor", lambda e, it=it, jt=jt, h=h, c=c, ab=ab, p_o=p_o, oof=oof,
                                                 st=(ix == 0), sp=(ix == nmm - 1):
                                                 e.matmul(p_o[:, oof + it * 256: oof + it * 256 + 256],
                                                          ab[:, jt * 256 + it * 128: jt * 256 + it * 128 + 128],
                                                          vtm[:, (c * 2 + jt) * D + h * 256:(c * 2 + jt) * D + (h + 1) * 256],
                                                          start=st, stop=sp),
                                                 inc="PE" if (ix == nmm - 1) else None)
                                    ix += 1
                                if n > 0:
                                    for dt_ in range(2):
                                        mo = sch.run("tensor", lambda e, it=it, dt_=dt_, h=h, qb=qb, p_o=p_o, oof=oof,
                                                     sp=(ix == nmm - 1):
                                                     e.matmul(p_o[:, oof + it * 256: oof + it * 256 + 256],
                                                              qb[:, dt_ * 256 + it * 128: dt_ * 256 + it * 128 + 128],
                                                              Sbf[:, h * 512 + dt_ * 256: h * 512 + (dt_ + 1) * 256],
                                                              start=False, stop=sp),
                                                     inc="PE" if (ix == nmm - 1) else None)
                                        ix += 1
                            prev_out_mm[u] = mo
                            prev_at_rd[u] = mo
                            sch.wait("tensor", "VE", mke[1])
                            sch.waits("tensor", [("VE", S_fma_mark[h][1])])
                            pswait("tensor", psS)
                            ms = None
                            for dt_ in range(2):
                                for jt in range(2):
                                    ms = sch.run("tensor", lambda e, dt_=dt_, jt=jt, h=h, c=c, tb_=tb_:
                                                 e.matmul(psS[:, dt_ * 256:(dt_ + 1) * 256],
                                                          tb_[:, (jt * 2 + dt_) * 128:(jt * 2 + dt_ + 1) * 128],
                                                          vtm[:, (c * 2 + jt) * D + h * 256:(c * 2 + jt) * D + (h + 1) * 256],
                                                          start=(jt == 0), stop=(jt == 1)),
                                                 inc="PE" if (dt_ == 1 and jt == 1) else None)
                            prev_S_mm[u] = ms
                            sch.wait("vector", "PE", mo[1])
                            sch.waits("vector", [("qX", xev[0][1]), ("qX", xev[1][1])])
                            moe = None
                            for it in range(2):
                                moe = sch.run("vector", lambda e, it=it, h=h, p_o=p_o, oof=oof:
                                              e.tensor_tensor(xc[it][:, h * 256:(h + 1) * 256],
                                                              xc[it][:, h * 256:(h + 1) * 256],
                                                              p_o[:, oof + it * 256: oof + it * 256 + 256], ALU.add),
                                              inc="VE")
                            psfree(p_o, moe)
                            out_evac_marks.append(moe)
                            sch.wait("vector", "PE", ms[1])
                            msf = None
                            for dt_ in range(2):
                                msf = sch.run("vector", lambda e, dt_=dt_, h=h:
                                              e.scalar_tensor_tensor(S32[:, h * 512 + dt_ * 256: h * 512 + (dt_ + 1) * 256],
                                                                     S32[:, h * 512 + dt_ * 256: h * 512 + (dt_ + 1) * 256],
                                                                     float(cdec[h]),
                                                                     psS[:, dt_ * 256:(dt_ + 1) * 256],
                                                                     ALU.mult, ALU.add),
                                              inc="VE")
                            psfree(psS, msf)
                            S_fma_mark[h] = msf
                            sch.wait("vector", "VE", msf[1])
                            msc2 = sch.run("vector", lambda e, h=h:
                                           e.tensor_copy(Sbf[:, h * 512:(h + 1) * 512],
                                                         S32[:, h * 512:(h + 1) * 512]),
                                           inc="VE")
                            S_copy_mark[h] = msc2
                        for it in range(2):
                            gi = qt * 4 + c * 2 + it
                            sch.wait("vector", "VE", out_evac_marks[-1][1])
                            mb = None
                            for i4 in range(4):
                                mb = sch.run("vector", lambda e, i4=i4, it=it:
                                             e.bn_stats(bnst2[:, i4 * 6:(i4 + 1) * 6],
                                                        xc[it][:, i4 * 512:(i4 + 1) * 512]),
                                             inc="VE" if i4 == 3 else None)
                            sch.wait("vector", "VE", mb[1])
                            mag = sch.run("vector", lambda e, gi=gi:
                                          e.bn_aggr(mvst[:, 2 * gi:2 * gi + 2], bnst2[:, 0:24]), inc="VE")
                            stats_ready[gi] = mag
                            deps = [("VE", out_evac_marks[-1][1])]
                            prev_store[it] = sch.dma("sync", x_cur[gi * 128:(gi + 1) * 128, :],
                                                     xc[it][:], deps)
                            chunk_xwar[it] = mag
                last_x_store = ("qX", sch.cnt["qX"])
            return

        def emit_ffn(call, src, dst):
            nonlocal last_x_store
            HS = S // 2
            ph = ExitStack()
            with ph:
                lnT2 = [ph.enter_context(nc.sbuf_tensor(f"lnTf{call}_{i}", [128, NK * HS], BF16))
                        for i in range(2)]
                acc = ph.enter_context(nc.sbuf_tensor(f"accf{call}", [128, 8 * D], BF16))
                comb = ph.enter_context(nc.sbuf_tensor(f"comb{call}", [128, NK * NE], F32))
                glog = ph.enter_context(nc.sbuf_tensor(f"glog{call}", [128, 2 * NE + 8], F32))
                wgu = [ph.enter_context(nc.sbuf_tensor(f"wgu{call}_{i}", [128, NK * 256], BF16))
                       for i in range(2)]
                wdn = [ph.enter_context(nc.sbuf_tensor(f"wdn{call}_{i}", [128, 8 * 1024], BF16))
                       for i in range(2)]
                intT = ph.enter_context(nc.sbuf_tensor(f"intT{call}", [128, 8 * HS], BF16))
                tsil = ph.enter_context(nc.sbuf_tensor(f"tsil{call}", [128, 1024], BF16))
                ytmp = ph.enter_context(nc.sbuf_tensor(f"ytmp{call}", [128, 1024], BF16))
                wgu_use = [[], []]
                wdn_use = [[], []]
                sch.wait("tensor", "qW", ev_gw[1])

                ln_done2 = []
                for th in range(2):
                    gis = list(range(th * 8, th * 8 + 8))
                    lnT = lnT2[th]
                    sm = stats_ready if (FUSE_STATS and all(gi in stats_ready for gi in gis)) else None
                    _, evacs, _ = emit_ln(src, gis, lnT, HS,
                                          lambda gi, th=th: gi * 128 - th * HS,
                                          stats_marks=sm)
                    ln_done2.append(evacs[gis[-1]])
                    for gi in gis:
                        gl = gi - th * 8
                        p = psf[gl % 2]
                        sch.waits("tensor", [evacs[gi]])
                        pswait("tensor", p)
                        mm = None
                        for k in range(NK):
                            mm = sch.run("tensor", lambda e, p=p, k=k, gl=gl, lnT=lnT:
                                         e.matmul(p[:, 0:NE],
                                                  lnT[:, k * HS + gl * 128: k * HS + gl * 128 + 128],
                                                  gwT_sb[:, k * NE:(k + 1) * NE],
                                                  start=(k == 0), stop=(k == NK - 1)),
                                         inc="PE" if k == NK - 1 else None)
                        lg = glog[:, 0:NE]; lg2 = glog[:, NE:2 * NE]
                        mmx = glog[:, 2 * NE:2 * NE + 2]; ssx = glog[:, 2 * NE + 2:2 * NE + 4]
                        den = glog[:, 2 * NE + 4:2 * NE + 5]; rr = glog[:, 2 * NE + 5:2 * NE + 6]
                        ww = glog[:, 2 * NE + 6:2 * NE + 8]
                        sch.wait("vector", "PE", mm[1])
                        mv = sch.run("vector", lambda e, p=p: e.tensor_copy(lg, p[:, 0:NE]), inc="VE")
                        psfree(p, mv)
                        sch.wait("vector", "VE", mv[1])
                        mv = sch.run("vector", lambda e: e.reduce_max(mmx[:, 0:1], lg, axis=AX.X), inc="VE")
                        sch.wait("vector", "VE", mv[1])
                        mv = sch.run("vector", lambda e: e.tensor_scalar(lg2, lg, mmx[:, 0:1], None, ALU.is_ge), inc="VE")
                        sch.wait("vector", "VE", mv[1])
                        mv = sch.run("vector", lambda e: e.scalar_tensor_tensor(lg2, lg2, -1e4, lg, ALU.mult, ALU.add), inc="VE")
                        sch.wait("vector", "VE", mv[1])
                        mv = sch.run("vector", lambda e: e.reduce_max(mmx[:, 1:2], lg2, axis=AX.X), inc="VE")
                        sch.wait("scalar", "VE", mv[1])
                        msig = sch.run("scalar", lambda e: e.activation(ssx, mmx, AF.Sigmoid), inc="SE")
                        sch.wait("vector", "SE", msig[1])
                        mv = sch.run("vector", lambda e: e.tensor_add(den, ssx[:, 0:1], ssx[:, 1:2]), inc="VE")
                        sch.wait("vector", "VE", mv[1])
                        mv = sch.run("vector", lambda e: e.reciprocal(rr, den), inc="VE")
                        sch.wait("vector", "VE", mv[1])
                        mv = sch.run("vector", lambda e: e.tensor_scalar(ww, ssx, rr, None, ALU.mult), inc="VE")
                        sch.wait("vector", "VE", mv[1])
                        mv = sch.run("vector", lambda e: e.tensor_scalar(lg, lg, mmx[:, 0:1], None, ALU.is_ge), inc="VE")
                        sch.wait("vector", "VE", mv[1])
                        mv = sch.run("vector", lambda e: e.tensor_scalar(lg2, lg2, mmx[:, 1:2], None, ALU.is_ge), inc="VE")
                        sch.wait("vector", "VE", mv[1])
                        mv = sch.run("vector", lambda e: e.tensor_scalar(lg, lg, ww[:, 0:1], None, ALU.mult), inc="VE")
                        sch.wait("vector", "VE", mv[1])
                        mv = sch.run("vector", lambda e, gi=gi: e.scalar_tensor_tensor(
                            comb[:, gi * NE:(gi + 1) * NE], lg2, ww[:, 1:2], lg, ALU.mult, ALU.add), inc="VE")
                gating_done = ("VE", sch.cnt["VE"])

                for th in range(2):
                    lnT = lnT2[th]
                    ln_done = ln_done2[th]
                    tsil_rd = [None, None]
                    ytmp_rd = [None, None]
                    sch.wait("scalar", "VE", sch.cnt["VE"])
                    first_pass = (call == 0 and th == 0)
                    for e_ix in range(NE):
                        guev = [None] * 8

                        def load_gu(mp, e_ix=e_ix):
                            slot = (th * 8 + e_ix * 8 + mp) % 2
                            dep = wgu_use[slot] if wgu_use[slot] else [ln_done2[1 if call == 0 else 0]]
                            if first_pass:
                                ev = sch.dma("gpsimd",
                                             wgu[slot].ap().rearrange("p (k j) -> p k j", k=NK),
                                             dr["Wgu"][e_ix, mp].rearrange("k p j -> p k j"),
                                             deps=dep)
                                sev = sch.dma("sync", wgu_c[e_ix, mp], wgu[slot][:], deps=[ev])
                                guev[mp] = (ev, "qW", sev)
                            else:
                                qq = "scalar" if slot == 0 else "sync"
                                ev = sch.dma(qq, wgu[slot][:], wgu_c[e_ix, mp], deps=dep)
                                guev[mp] = (ev, sch.QSEM[qq], None)

                        load_gu(0)
                        load_gu(1)
                        dnev = []
                        for hh in range(2):
                            slot = (th * 2 + e_ix * 2 + hh) % 2
                            dep = wdn_use[slot] if wdn_use[slot] else [ln_done2[1 if call == 0 else 0]]
                            if first_pass:
                                ev = sch.dma("gpsimd",
                                             wdn[slot].ap().rearrange("p (k j) -> p k j", k=8),
                                             dr["Wdn"][e_ix, hh].rearrange("k p j -> p k j"),
                                             deps=dep)
                                sev = sch.dma("sync", wdn_c[e_ix, hh], wdn[slot][:], deps=[ev])
                                dnev.append((ev, "qW", sev))
                            else:
                                ev = sch.dma("scalar", wdn[slot][:], wdn_c[e_ix, hh], deps=dep)
                                dnev.append((ev, "qS", None))
                        int_marks = []
                        for mp in range(8):
                            slot = (th * 8 + e_ix * 8 + mp) % 2
                            for tb in range(2):
                                pg = psf[0] if tb % 2 == 0 else psf[1]
                                pu = psf[2] if tb % 2 == 0 else psf[3]
                                puap = pu[:]
                                sch.wait("tensor", guev[mp][1], guev[mp][0][1])
                                pswait("tensor", pg); pswait("tensor", pu)
                                mmg = mmu = None
                                for k in range(NK):
                                    mmg = sch.run("tensor", lambda e, pg=pg, k=k, tb=tb, slot=slot, lnT=lnT:
                                                  e.matmul(pg[:], wgu[slot][:, k * 256:k * 256 + 128],
                                                           lnT[:, k * HS + tb * 512:k * HS + (tb + 1) * 512],
                                                           start=(k == 0), stop=(k == NK - 1)),
                                                  inc="PE" if k == NK - 1 else None)
                                    mmu = sch.run("tensor", lambda e, puap=puap, k=k, tb=tb, slot=slot, lnT=lnT:
                                                  e.matmul(puap, wgu[slot][:, k * 256 + 128:k * 256 + 256],
                                                           lnT[:, k * HS + tb * 512:k * HS + (tb + 1) * 512],
                                                           start=(k == 0), stop=(k == NK - 1)),
                                                  inc="PE" if k == NK - 1 else None)
                                if tb == 1:
                                    wgu_use[slot] = [mmu] + ([guev[mp][2]] if guev[mp][2] else [])
                                    if mp + 2 < 8:
                                        load_gu(mp + 2)
                                ts_slot = (mp * 2 + tb) % 2
                                sch.wait("scalar", "PE", mmg[1])
                                sch.waits("scalar", [tsil_rd[ts_slot]])
                                msl = sch.run("scalar", lambda e, pg=pg, sl=ts_slot:
                                              e.activation(tsil[:, sl * 512:(sl + 1) * 512], pg[:], AF.Silu),
                                              inc="SE")
                                psfree(pg, msl)
                                sch.wait("vector", "SE", msl[1])
                                sch.wait("vector", "PE", mmu[1])
                                mi = sch.run("vector", lambda e, puap=puap, mp=mp, tb=tb, sl=ts_slot:
                                             e.tensor_tensor(intT[:, mp * HS + tb * 512: mp * HS + (tb + 1) * 512],
                                                             tsil[:, sl * 512:(sl + 1) * 512], puap, ALU.mult),
                                             inc="VE")
                                tsil_rd[ts_slot] = mi
                                psfree(pu, mi)
                                int_marks.append(mi)
                        int_done = int_marks[-1]
                        for hh in range(2):
                            slot = (th * 2 + e_ix * 2 + hh) % 2
                            last_y = None
                            for tt in range(8):
                                for jb in range(2):
                                    p = psf[(tt * 2 + jb) % 4]
                                    sch.wait("tensor", dnev[hh][1], dnev[hh][0][1])
                                    sch.waits("tensor", [("VE", int_done[1])])
                                    pswait("tensor", p)
                                    mm = None
                                    for k in range(8):
                                        mm = sch.run("tensor", lambda e, p=p, k=k, tt=tt, jb=jb, slot=slot:
                                                     e.matmul(p[:], intT[:, k * HS + tt * 128: k * HS + tt * 128 + 128],
                                                              wdn[slot][:, k * 1024 + jb * 512: k * 1024 + (jb + 1) * 512],
                                                              start=(k == 0), stop=(k == 7)),
                                                     inc="PE" if k == 7 else None)
                                    last_y = mm
                                    sch.wait("scalar", "PE", mm[1])
                                    sch.waits("scalar", [gating_done])
                                    c_ap = comb[:, (th * 8 + tt) * NE + e_ix: (th * 8 + tt) * NE + e_ix + 1]
                                    if e_ix == 0:
                                        mev = sch.run("scalar", lambda e, p=p, tt=tt, hh=hh, jb=jb, c_ap=c_ap:
                                                      e.activation(acc[:, tt * D + hh * 1024 + jb * 512:
                                                                       tt * D + hh * 1024 + (jb + 1) * 512],
                                                                   p[:], AF.Identity, scale=c_ap),
                                                      inc="SE")
                                        psfree(p, mev)
                                    else:
                                        y_slot = (tt * 2 + jb) % 2
                                        sch.waits("scalar", [ytmp_rd[y_slot]])
                                        mev = sch.run("scalar", lambda e, p=p, c_ap=c_ap, sl=y_slot:
                                                      e.activation(ytmp[:, sl * 512:(sl + 1) * 512],
                                                                   p[:], AF.Identity, scale=c_ap),
                                                      inc="SE")
                                        psfree(p, mev)
                                        sch.wait("vector", "SE", mev[1])
                                        ma = sch.run("vector", lambda e, tt=tt, hh=hh, jb=jb, sl=y_slot:
                                                     e.tensor_tensor(acc[:, tt * D + hh * 1024 + jb * 512:
                                                                         tt * D + hh * 1024 + (jb + 1) * 512],
                                                                     acc[:, tt * D + hh * 1024 + jb * 512:
                                                                         tt * D + hh * 1024 + (jb + 1) * 512],
                                                                     ytmp[:, sl * 512:(sl + 1) * 512], ALU.add),
                                                     inc="VE")
                                        ytmp_rd[y_slot] = ma
                            wdn_use[slot] = [last_y] + ([dnev[hh][2]] if dnev[hh][2] else [])
                    acc_done_v = ("VE", sch.cnt["VE"])
                    acc_done_s = ("SE", sch.cnt["SE"])
                    prev = [None, None]
                    for tt in range(8):
                        gi = th * 8 + tt
                        deps = list(xt_last[tt % 2])
                        if prev[tt % 2] is not None:
                            deps.append(prev[tt % 2])
                        ev = sch.dma("sync", xt[tt % 2][:], src[gi * 128:(gi + 1) * 128, :], deps)
                        sch.wait("vector", "qX", ev[1])
                        sch.waits("vector", [acc_done_v, acc_done_s])
                        ma = sch.run("vector", lambda e, tt=tt:
                                     e.tensor_tensor(xt[tt % 2][:], xt[tt % 2][:],
                                                     acc[:, tt * D:(tt + 1) * D], ALU.add),
                                     inc="VE")
                        xt_last[tt % 2] = [ma]
                        if call == 0:
                            mb = None
                            for i4 in range(4):
                                mb = sch.run("vector", lambda e, i4=i4, tt=tt:
                                             e.bn_stats(bnst2[:, i4 * 6:(i4 + 1) * 6],
                                                        xt[tt % 2][:, i4 * 512:(i4 + 1) * 512]),
                                             inc="VE" if i4 == 3 else None)
                            sch.wait("vector", "VE", mb[1])
                            mag = sch.run("vector", lambda e, gi=gi:
                                          e.bn_aggr(mvst[:, 2 * gi:2 * gi + 2], bnst2[:, 0:24]), inc="VE")
                            stats_ready[gi] = mag
                            xt_last[tt % 2] = [mag]
                        prev[tt % 2] = sch.dma("sync", dst[gi * 128:(gi + 1) * 128, :], xt[tt % 2][:],
                                               deps=[ma])
                    last_x_store = ("qX", sch.cnt["qX"])

        def emit_attn_b(src, dst):
            nonlocal last_x_store
            HS = S // 2
            ph = ExitStack()
            with ph:
                attnT = ph.enter_context(nc.sbuf_tensor("attnTb", [128, NK * HS], BF16))
                lnT = ph.enter_context(nc.sbuf_tensor("lnTb", [128, NK * HS], BF16))
                kTb = ph.enter_context(nc.sbuf_tensor("kTb", [128, 4 * S], BF16))
                vb = ph.enter_context(nc.sbuf_tensor("vbb", [128, NK * 512], BF16))
                qTh = [ph.enter_context(nc.sbuf_tensor(f"qThb{i}", [128, HS], BF16)) for i in range(2)]
                wqcb = [ph.enter_context(nc.sbuf_tensor(f"wqcbb{i}", [128, NK * 128], BF16)) for i in range(2)]
                wkcb = [ph.enter_context(nc.sbuf_tensor(f"wkcbb{i}", [128, NK * 128], BF16)) for i in range(2)]
                wvch = [ph.enter_context(nc.sbuf_tensor(f"wvchb{i}", [128, 2 * 512], BF16)) for i in range(2)]
                wocb = [ph.enter_context(nc.sbuf_tensor(f"wocbb{i}", [128, NK * 512], BF16)) for i in range(2)]
                xcb = [ph.enter_context(nc.sbuf_tensor(f"xcb{i}", [128, 512], F32)) for i in range(2)]
                probsA = ph.enter_context(nc.sbuf_tensor("probsAb", [128, 8 * 640], BF16))
                maskA = ph.enter_context(nc.sbuf_tensor("maskAb", [128, 8 * 256], BF16))
                rbcA = ph.enter_context(nc.sbuf_tensor("rbcAb", [128, 8 * 128], F32))
                rrow4 = ph.enter_context(nc.sbuf_tensor("rrow4b", [1, 512], F32))
                tri1_sb = ph.enter_context(nc.sbuf_tensor("tri1_sb", [128, 128], BF16))
                tri2_sb = ph.enter_context(nc.sbuf_tensor("tri2_sb", [128, 128], BF16))
                pc = []
                for name, t in (("tri1", tri1_sb), ("tri2", tri2_sb)):
                    pc.append(sch.dma("sync", t[:], dr[name]))
                sch.wait("vector", "qX", pc[-1][1])

                wk_use = [[], []]
                wv_use = [[], []]
                wq_use = [[], []]
                wo_use = [[], []]
                rbc_war = None
                prev_xst = [None, None]
                for th in range(2):
                    gis = list(range(th * 8, th * 8 + 8))
                    sm = stats_ready if (FUSE_STATS and all(gi in stats_ready for gi in gis)) else None
                    _, evacs, _ = emit_ln(src, gis, lnT, HS,
                                          lambda gi, th=th: gi * 128 - th * HS,
                                          stats_marks=sm)
                    ln_done = evacs[gis[-1]]

                    for jb in range(4):
                        slot = jb % 2
                        dep = wk_use[slot] if wk_use[slot] else [ln_done]
                        if th == 0:
                            ev = sch.dma("gpsimd", wkcb[slot].ap().rearrange("p (k j) -> p k j", k=NK),
                                         dr["Wkb"][jb].rearrange("k p j -> p k j"), deps=dep)
                            ksev = sch.dma("sync", wkb_c[jb], wkcb[slot][:], deps=[ev])
                            kwsem = "qW"
                        else:
                            qq = "scalar" if slot == 0 else "sync"
                            ev = sch.dma(qq, wkcb[slot][:], wkb_c[jb], deps=dep)
                            ksev = None
                            kwsem = sch.QSEM[qq]
                        for tb in range(2):
                            p = psf[(jb * 2 + tb) % 3]
                            sch.waits("tensor", [ln_done, (kwsem, ev[1])])
                            pswait("tensor", p)
                            mm = None
                            for k in range(NK):
                                mm = sch.run("tensor", lambda e, p=p, k=k, tb=tb, slot=slot:
                                             e.matmul(p[:], wkcb[slot][:, k * 128:(k + 1) * 128],
                                                      lnT[:, k * HS + tb * 512: k * HS + (tb + 1) * 512],
                                                      start=(k == 0), stop=(k == NK - 1)),
                                             inc="PE" if k == NK - 1 else None)
                            wk_use[slot] = [mm] + ([ksev] if ksev else [])
                            sch.wait("vector", "PE", mm[1])
                            me = sch.run("vector", lambda e, p=p, jb=jb, tb=tb, th=th:
                                         e.tensor_copy(kTb[:, jb * S + th * HS + tb * 512:
                                                           jb * S + th * HS + (tb + 1) * 512], p[:]),
                                         inc="VE")
                            psfree(p, me)
                    for ttg in range(2):
                        mmq = [None] * 4
                        for k2 in range(8):
                            slot = k2 % 2
                            dep = wv_use[slot] if wv_use[slot] else [ln_done]
                            if th == 0 and ttg == 0:
                                ev = sch.dma("gpsimd", wvch[slot].ap().rearrange("p (k c) -> p k c", k=2),
                                             dr["Wvb"][2 * k2:2 * k2 + 2].rearrange("k p c -> p k c"),
                                             deps=dep)
                                vsev = sch.dma("sync", wvb_c[k2], wvch[slot][:], deps=[ev])
                                vwsem = "qW"
                            else:
                                qq = "scalar" if slot == 0 else "sync"
                                ev = sch.dma(qq, wvch[slot][:], wvb_c[k2], deps=dep)
                                vsev = None
                                vwsem = sch.QSEM[qq]
                            sch.waits("tensor", [ln_done, (vwsem, ev[1])])
                            for ti in range(4):
                                ttl = ttg * 4 + ti
                                p = psf[ti] if ti < 3 else psb2
                                pap = p[:] if ti < 3 else p[:, 0:512]
                                if k2 == 0:
                                    pswait("tensor", p)
                                mm = None
                                for kk in range(2):
                                    k = 2 * k2 + kk
                                    mark = (k == NK - 1) or (ti == 3 and kk == 1)
                                    mm = sch.run("tensor", lambda e, pap=pap, k=k, kk=kk, ttl=ttl, slot=slot:
                                                 e.matmul(pap, lnT[:, k * HS + ttl * 128: k * HS + ttl * 128 + 128],
                                                          wvch[slot][:, kk * 512:(kk + 1) * 512],
                                                          start=(k == 0), stop=(k == NK - 1)),
                                                 inc="PE" if mark else None)
                                mmq[ti] = mm
                            wv_use[slot] = [mmq[3]] + ([vsev] if vsev else [])
                        for ti in range(4):
                            ttl = ttg * 4 + ti
                            tt = th * 8 + ttl
                            p = psf[ti] if ti < 3 else psb2
                            pap = p[:] if ti < 3 else p[:, 0:512]
                            sch.wait("vector", "PE", mmq[ti][1])
                            me = sch.run("vector", lambda e, pap=pap, tt=tt:
                                         e.tensor_copy(vb[:, tt * 512:(tt + 1) * 512], pap), inc="VE")
                            psfree(p, me)
                    kv_done = ("VE", sch.cnt["VE"])

                    def emit_qproj(h):
                        wslot = h % 2
                        dep = wq_use[wslot] if wq_use[wslot] else [ln_done]
                        if th == 0:
                            ev = sch.dma("gpsimd", wqcb[wslot].ap().rearrange("p (k j) -> p k j", k=NK),
                                         dr["Wqb"][h].rearrange("k p j -> p k j"), deps=dep)
                            qsev = sch.dma("sync", wqb_c[h], wqcb[wslot][:], deps=[ev])
                            qwsem = "qW"
                        else:
                            qq = "scalar" if wslot == 0 else "sync"
                            ev = sch.dma(qq, wqcb[wslot][:], wqb_c[h], deps=dep)
                            qsev = None
                            qwsem = sch.QSEM[qq]
                        for tb in range(2):
                            p = psf[2]
                            sch.waits("tensor", [ln_done, (qwsem, ev[1])])
                            pswait("tensor", p)
                            mm = None
                            for k in range(NK):
                                mm = sch.run("tensor", lambda e, p=p, k=k, tb=tb, wslot=wslot:
                                             e.matmul(p[:], wqcb[wslot][:, k * 128:(k + 1) * 128],
                                                      lnT[:, k * HS + tb * 512: k * HS + (tb + 1) * 512],
                                                      start=(k == 0), stop=(k == NK - 1)),
                                             inc="PE" if k == NK - 1 else None)
                            wq_use[wslot] = [mm] + ([qsev] if qsev else [])
                            sch.wait("vector", "PE", mm[1])
                            me = sch.run("vector", lambda e, tb=tb, wslot=wslot:
                                         e.tensor_copy(qTh[wslot][:, tb * 512:(tb + 1) * 512], p[:]),
                                         inc="VE")
                            psfree(psf[2], me)
                        return ("VE", sch.cnt["VE"])

                    qready = emit_qproj(0)
                    for h in range(HQ):
                        kvh = h // 4
                        hs = h % 2
                        qT_h = qTh[hs]

                        def kts_of(ql):
                            qt = th * 8 + ql
                            return [kt for kt in range(qt - 4, qt + 1) if kt >= 0]

                        mask_done = [None] * 8
                        for ql in range(8):
                            kts = kts_of(ql)
                            nsl = len(kts)
                            has2 = nsl == 5
                            if ql % 2 == 0:
                                regions = [(psb2, 0, 0, min(nsl, 8))]
                            else:
                                regions = [(psf[0], 0, 0, min(nsl, 4))]
                                if nsl > 4:
                                    regions.append((psf[1], 0, 4, nsl - 4))
                            sch.waits("tensor", [qready, kv_done])
                            mmlast = None
                            for (pt, base, lo, ns) in regions:
                                pswait("tensor", pt)
                                for s_ix in range(lo, lo + ns):
                                    kt = kts[s_ix]
                                    col = base + (s_ix - lo) * 128
                                    mmlast = sch.run("tensor", lambda e, pt=pt, col=col, kt=kt, ql=ql, kvh=kvh, qT_h=qT_h:
                                            e.matmul(pt[:, col:col + 128],
                                                     kTb[:, kvh * S + kt * 128: kvh * S + (kt + 1) * 128],
                                                     qT_h[:, ql * 128:(ql + 1) * 128],
                                                     start=True, stop=True),
                                            inc="PE")
                            sch.wait("scalar", "PE", mmlast[1])
                            mex = None
                            dst_of = ql * 640
                            for (pt, base, lo, ns) in regions:
                                mex = sch.run("scalar", lambda e, pt=pt, base=base, lo=lo, ns=ns, dst_of=dst_of:
                                              e.activation(probsA[:, dst_of + lo * 128: dst_of + (lo + ns) * 128],
                                                           pt[:, base:base + ns * 128],
                                                           AF.Exp, scale=1.0 / 128.0),
                                              inc="SE")
                                psfree(pt, mex)
                            sch.wait("vector", "SE", mex[1])
                            mm1 = sch.run("vector", lambda e, ql=ql, nsl=nsl:
                                          e.tensor_tensor(maskA[:, ql * 256:ql * 256 + 128],
                                                          probsA[:, ql * 640 + (nsl - 1) * 128: ql * 640 + nsl * 128],
                                                          tri1_sb[:], ALU.mult), inc="VE")
                            if has2:
                                mm1 = sch.run("vector", lambda e, ql=ql:
                                              e.tensor_tensor(maskA[:, ql * 256 + 128:ql * 256 + 256],
                                                              probsA[:, ql * 640: ql * 640 + 128],
                                                              tri2_sb[:], ALU.mult), inc="VE")
                            mask_done[ql] = mm1

                        def slot_ap(ql, s_ix):
                            kts = kts_of(ql)
                            nsl = len(kts)
                            has2 = nsl == 5
                            if s_ix == nsl - 1:
                                return maskA[:, ql * 256:ql * 256 + 128]
                            if has2 and s_ix == 0:
                                return maskA[:, ql * 256 + 128:ql * 256 + 256]
                            return probsA[:, ql * 640 + s_ix * 128: ql * 640 + (s_ix + 1) * 128]

                        if h + 1 < HQ:
                            qready = emit_qproj(h + 1)

                        for r4 in range(2):
                            msum = None
                            pswait("tensor", pssm)
                            for qi in range(4):
                                ql = r4 * 4 + qi
                                kts = kts_of(ql)
                                nsl = len(kts)
                                sch.waits("tensor", [("VE", mask_done[ql][1])])
                                for s_ix in range(nsl):
                                    sp_ = s_ix == nsl - 1
                                    msum = sch.run("tensor", lambda e, qi=qi, ap=slot_ap(ql, s_ix),
                                                   st=(s_ix == 0), sp=sp_:
                                                   e.matmul(pssm[0:1, qi * 128:(qi + 1) * 128], ones128[:], ap,
                                                            start=st, stop=sp),
                                                   inc="PE" if sp_ else None)
                            sch.wait("vector", "PE", msum[1])
                            mrc = sch.run("vector", lambda e: e.reciprocal(rrow4[:], pssm[0:1, 0:512]), inc="VE")
                            psfree(pssm, mrc)
                            p_rb = psf[0]
                            sch.wait("tensor", "VE", mrc[1])
                            pswait("tensor", p_rb)
                            mmr = sch.run("tensor", lambda e, p_rb=p_rb:
                                          e.matmul(p_rb[:], ones1[:], rrow4[0:1, 0:512],
                                                   start=True, stop=True),
                                          inc="PE")
                            sch.wait("vector", "PE", mmr[1])
                            mrb = sch.run("vector", lambda e, r4=r4, p_rb=p_rb:
                                          e.tensor_copy(rbcA[:, r4 * 512:(r4 + 1) * 512], p_rb[:]), inc="VE")
                            psfree(p_rb, mrb)
                        rbc_done = ("VE", sch.cnt["VE"])

                        for r4 in range(2):
                            p_o = psf[2 + r4]
                            pswait("tensor", p_o)
                            mmo = None
                            for qi in range(4):
                                ql = r4 * 4 + qi
                                kts = kts_of(ql)
                                nsl = len(kts)
                                sch.waits("tensor", [("VE", mask_done[ql][1]), kv_done])
                                for s_ix in range(nsl):
                                    kt = kts[s_ix]
                                    sp_ = s_ix == nsl - 1
                                    mmo = sch.run("tensor", lambda e, qi=qi, kt=kt, kvh=kvh, ap=slot_ap(ql, s_ix),
                                                  st=(s_ix == 0), sp=sp_, p_o=p_o:
                                                  e.matmul(p_o[:, qi * 128:(qi + 1) * 128],
                                                           vb[:, kt * 512 + kvh * 128: kt * 512 + (kvh + 1) * 128],
                                                           ap, start=st, stop=sp),
                                                  inc="PE" if sp_ else None)
                            sch.wait("vector", "PE", mmo[1])
                            sch.waits("vector", [rbc_done])
                            moe = sch.run("vector", lambda e, h=h, r4=r4, p_o=p_o:
                                          e.tensor_tensor(attnT[:, h * HS + r4 * 512: h * HS + (r4 + 1) * 512],
                                                          p_o[:], rbcA[:, r4 * 512:(r4 + 1) * 512], ALU.mult),
                                          inc="VE")
                            psfree(p_o, moe)
                            rbc_war = moe
                    attnT_done = ("VE", sch.cnt["VE"])
                    if DBG_ATTNB and th == 0:
                        sch.dma("sync", dbg["attnT"], attnT[:], deps=[attnT_done])
                        sch.dma("sync", dbg["negm"], negm[:], deps=[attnT_done])
                        sch.dma("sync", dbg["kTb"], kTb[:], deps=[attnT_done])
                        sch.dma("sync", dbg["vb"], vb[:], deps=[attnT_done])
                        sch.dma("sync", dbg["qTh"], qTh[:], deps=[attnT_done])

                    for jb in range(4):
                        slot = jb % 2
                        dep = wo_use[slot] if wo_use[slot] else [attnT_done]
                        if th == 0:
                            ev = sch.dma("gpsimd", wocb[slot].ap().rearrange("p (k j) -> p k j", k=NK),
                                         dr["Wob"][jb].rearrange("k p j -> p k j"), deps=dep)
                            osev = sch.dma("sync", wob_c[jb], wocb[slot][:], deps=[ev])
                            owsem = "qW"
                        else:
                            qq = "scalar" if slot == 0 else "sync"
                            ev = sch.dma(qq, wocb[slot][:], wob_c[jb], deps=dep)
                            osev = None
                            owsem = sch.QSEM[qq]
                        for ttl in range(8):
                            gi = th * 8 + ttl
                            bslot = ttl % 2
                            xev = sch.dma("sync", xcb[bslot][:, 0:512],
                                          src[gi * 128:(gi + 1) * 128, jb * 512:(jb + 1) * 512],
                                          deps=[prev_xst[bslot] if prev_xst[bslot] is not None else attnT_done])
                            p = psf[(jb * 8 + ttl) % 3]
                            sch.waits("tensor", [attnT_done, (owsem, ev[1])])
                            pswait("tensor", p)
                            mm = None
                            for k in range(NK):
                                mm = sch.run("tensor", lambda e, p=p, k=k, ttl=ttl, slot=slot:
                                             e.matmul(p[:], attnT[:, k * HS + ttl * 128: k * HS + ttl * 128 + 128],
                                                      wocb[slot][:, k * 512:(k + 1) * 512],
                                                      start=(k == 0), stop=(k == NK - 1)),
                                             inc="PE" if k == NK - 1 else None)
                            wo_use[slot] = [mm] + ([osev] if osev else [])
                            sch.wait("vector", "PE", mm[1])
                            sch.wait("vector", "qX", xev[1])
                            ma = sch.run("vector", lambda e, p=p, bslot=bslot:
                                         e.tensor_tensor(xcb[bslot][:, 0:512], xcb[bslot][:, 0:512],
                                                         p[:], ALU.add),
                                         inc="VE")
                            psfree(p, ma)
                            sch.wait("vector", "VE", ma[1])
                            mb = sch.run("vector", lambda e, ttl=ttl, jb=jb, bslot=bslot:
                                         e.bn_stats(bnst3[:, ttl * 24 + jb * 6: ttl * 24 + (jb + 1) * 6],
                                                    xcb[bslot][:, 0:512]),
                                         inc="VE")
                            if jb == 3:
                                sch.wait("vector", "VE", mb[1])
                                mag = sch.run("vector", lambda e, gi=gi, ttl=ttl:
                                              e.bn_aggr(mvst[:, 2 * gi:2 * gi + 2],
                                                        bnst3[:, ttl * 24:(ttl + 1) * 24]), inc="VE")
                                stats_ready[gi] = mag
                            prev_xst[bslot] = sch.dma("sync", dst[gi * 128:(gi + 1) * 128, jb * 512:(jb + 1) * 512],
                                                      xcb[bslot][:, 0:512], deps=[ma, mb])
            last_x_store = ("qX", sch.cnt["qX"])

        srcs = [x_in, x_cur, x_cur, x_cur]
        if nsub >= 1:
            emit_attn_a()
        if nsub >= 2:
            emit_ffn(0, x_cur if nsub >= 2 else x_in, x_cur)
        if nsub >= 3:
            emit_attn_b(x_cur, x_cur)
        if nsub >= 4:
            emit_ffn(1, x_cur, y_out)
        else:
            src = x_cur if nsub >= 1 else x_in
            sch.waits("sync", [last_x_store] if last_x_store else [])
            sch.dma("sync", y_out, src)

        sch.finish()

        with nc.Block() as block:
            @block.sync
            def _(e):
                for fn in sch.ops["sync"]:
                    fn(e)

            @block.gpsimd
            def _(e):
                for fn in sch.ops["gpsimd"]:
                    fn(e)

            @block.tensor
            def _(e):
                for fn in sch.ops["tensor"]:
                    fn(e)

            @block.vector
            def _(e):
                for fn in sch.ops["vector"]:
                    fn(e)

            @block.scalar
            def _(e):
                for fn in sch.ops["scalar"]:
                    fn(e)

    return nc


def kernel(**inputs):
    consts = host_constants()
    w = relayout_weights(inputs)
    nc = build_program(NSUB)
    base = {k: v for k, v in w.items()}
    for k in ("ident_bf", "ones1_bf", "ones1_f32", "ones128_bf", "DT",
              "cs_bc", "kv_bc", "tri1", "tri2"):
        base[k] = consts[k]
    x = inputs["L0_attn_a_x"]
    in_maps = [dict(base, x=np.ascontiguousarray(x[b])) for b in range(8)]
    res = run_bass_kernel_spmd(nc, in_maps, core_ids=list(range(8)))
    return np.stack([res.results[b]["y"] for b in range(8)], axis=0)


if __name__ == "__main__":
    rng = np.random.default_rng(0)
    fake = {
        "L0_attn_a_x": rng.standard_normal((8, S, D), dtype=np.float32),
    }
    print("smoke build only")
    build_program(1)
    print("built ok")


# version 12
# speedup vs baseline: 1.0951x; 1.0951x over previous
import numpy as np
import ml_dtypes
from contextlib import ExitStack

import concourse.bass as bass
import concourse.mybir as mybir
from concourse.bass_utils import run_bass_kernel_spmd

F32 = mybir.dt.float32
BF16 = mybir.dt.bfloat16
AF = mybir.ActivationFunctionType
ALU = mybir.AluOpType
AX = mybir.AxisListType

S = 2048
D = 2048
NK = 16
HEADS_A = 8
DH_A = 256
CHUNK = 256
NCH = S // CHUNK
HQ, HKV, HD, WIN = 16, 4, 128, 512
NE, NI = 16, 1024
NSUB = 4
FUSE_STATS = True
USE_WCACHE = True
DBG_ATTNB = False

BF = ml_dtypes.bfloat16


class Sched:
    ENGS = ("sync", "gpsimd", "tensor", "vector", "scalar")

    def __init__(self, nc, es):
        self.nc = nc
        self.ops = {e: [] for e in self.ENGS}
        self.sem = {}
        self.cnt = {}
        self.lastwait = {e: {} for e in self.ENGS}
        for name in ("qW", "qX", "qS", "PE", "VE", "SE"):
            self.sem[name] = es.enter_context(nc.semaphore(name))
            self.cnt[name] = 0

    def wait(self, eng, sname, val):
        if val is None or val <= 0:
            return
        lw = self.lastwait[eng]
        if lw.get(sname, 0) >= val:
            return
        lw[sname] = val
        s = self.sem[sname]
        self.ops[eng].append(lambda e, s=s, v=val: e.wait_ge(s, v))

    def waits(self, eng, deps):
        for d in deps:
            if d is not None:
                self.wait(eng, d[0], d[1])

    def run(self, eng, fn, inc=None, step=1):
        if inc is None:
            self.ops[eng].append(fn)
            return None
        self.cnt[inc] += step
        v = self.cnt[inc]
        s = self.sem[inc]
        self.ops[eng].append(lambda e, fn=fn, s=s, st=step: fn(e).then_inc(s, st))
        return (inc, v)

    QSEM = {"gpsimd": "qW", "sync": "qX", "scalar": "qS"}

    def dma(self, q, out, in_, deps=()):
        self.waits(q, deps)
        sem = self.QSEM[q]
        return self.run(q, lambda e, o=out, i=in_: e.dma_start(out=o, in_=i),
                        inc=sem, step=16)

    def finish(self):
        self.wait("sync", "qX", self.cnt["qX"])
        self.wait("gpsimd", "qW", self.cnt["qW"])
        self.wait("scalar", "qS", self.cnt["qS"])


def host_constants():
    c = {}
    c["ident_bf"] = np.eye(128, dtype=BF)
    c["ident_f32"] = np.eye(128, dtype=np.float32)
    c["ones1_f32"] = np.ones((1, 128), dtype=np.float32)
    c["ones1_bf"] = np.ones((1, 128), dtype=BF)
    c["ones128_bf"] = np.ones((128, 1), dtype=BF)

    betas = np.linspace(np.log2(1.0 / 32.0), np.log2(1.0 / 512.0), HEADS_A)
    g = (1.0 - np.exp2(betas)).astype(np.float64)
    idx = np.arange(CHUNK)
    i_, j_ = idx[:, None], idx[None, :]
    DT = np.zeros((128, HEADS_A * 2 * CHUNK), dtype=np.float32)
    for h in range(HEADS_A):
        Dh = np.where(i_ >= j_, g[h] ** (i_ - j_).astype(np.float64), 0.0)
        DhT = Dh.T.astype(np.float32)
        for jt in range(2):
            DT[:, (h * 2 + jt) * CHUNK:(h * 2 + jt + 1) * CHUNK] = DhT[jt * 128:(jt + 1) * 128, :]
    c["DT"] = DT.astype(BF)
    cs = np.zeros((128, HEADS_A * CHUNK), dtype=np.float32)
    kv = np.zeros((128, HEADS_A * CHUNK), dtype=np.float32)
    for h in range(HEADS_A):
        cs[:, h * CHUNK:(h + 1) * CHUNK] = (g[h] ** (idx + 1).astype(np.float64))[None, :]
        kv[:, h * CHUNK:(h + 1) * CHUNK] = (g[h] ** (CHUNK - 1 - idx).astype(np.float64))[None, :]
    c["cs_bc"] = cs.astype(BF)
    c["kv_bc"] = kv.astype(BF)
    c["chunk_decay"] = (g ** CHUNK).astype(np.float64)

    k_, q_ = np.arange(128)[:, None], np.arange(128)[None, :]
    c["tri1"] = (k_ <= q_).astype(BF)
    c["tri2"] = (k_ >= q_ + 1).astype(BF)
    return c


def relayout_weights(inp):
    w = {}
    w["Wqa"] = np.ascontiguousarray(
        inp["L0_attn_a_Wq"].reshape(NK, 128, NK, 128).transpose(2, 0, 1, 3))
    w["Wka"] = np.ascontiguousarray(
        inp["L0_attn_a_Wk"].reshape(NK, 128, NK, 128).transpose(2, 0, 1, 3))
    w["Wva"] = np.ascontiguousarray(
        inp["L0_attn_a_Wv"].reshape(NK, 128, 4, 512).transpose(2, 0, 1, 3))
    w["bqa"] = np.ascontiguousarray(inp["L0_attn_a_bq"].reshape(NK, 128).T)
    w["bka"] = np.ascontiguousarray(inp["L0_attn_a_bk"].reshape(NK, 128).T)
    w["bva"] = np.ascontiguousarray(inp["L0_attn_a_bv"].reshape(1, D))

    w["Wqb"] = np.ascontiguousarray(
        inp["L0_attn_b_Wq"].reshape(NK, 128, NK, 128).transpose(2, 0, 1, 3))
    w["Wkb"] = np.ascontiguousarray(
        inp["L0_attn_b_Wk"].reshape(NK, 128, 4, 128).transpose(2, 0, 1, 3))
    w["Wvb"] = np.ascontiguousarray(inp["L0_attn_b_Wv"].reshape(NK, 128, 512))
    w["Wob"] = np.ascontiguousarray(
        inp["L0_attn_b_Wo"].reshape(NK, 128, 4, 512).transpose(2, 0, 1, 3))

    gup = inp["L0_ffn_gate_up_proj"]
    gg = gup[:, :, :NI].reshape(NE, NK, 128, 8, 128).transpose(0, 3, 1, 2, 4)
    uu = gup[:, :, NI:].reshape(NE, NK, 128, 8, 128).transpose(0, 3, 1, 2, 4)
    w["Wgu"] = np.ascontiguousarray(np.concatenate([gg, uu], axis=-1))
    w["Wdn"] = np.ascontiguousarray(
        inp["L0_ffn_down_proj"].reshape(NE, 8, 128, 2, 1024).transpose(0, 3, 1, 2, 4))
    w["gwT"] = np.ascontiguousarray(inp["L0_ffn_gate_w"].T.reshape(NK, 128, NE))
    return w


def build_program(nsub=NSUB):
    global DBG_ATTNB, FUSE_STATS, USE_WCACHE
    nc = bass.Bass("TRN2", target_bir_lowering=False)
    dr = {}

    def din(name, shape, dtype=F32):
        dr[name] = nc.dram_tensor(name, shape, dtype, kind="ExternalInput").ap()
        return dr[name]

    x_in = din("x", [S, D])
    din("Wqa", [NK, NK, 128, 128]); din("Wka", [NK, NK, 128, 128])
    din("Wva", [4, NK, 128, 512])
    din("bqa", [128, NK]); din("bka", [128, NK]); din("bva", [1, D])
    din("Wqb", [NK, NK, 128, 128]); din("Wkb", [4, NK, 128, 128])
    din("Wvb", [NK, 128, 512]); din("Wob", [4, NK, 128, 512])
    din("Wgu", [NE, 8, NK, 128, 256]); din("Wdn", [NE, 2, 8, 128, 1024])
    din("gwT", [NK, 128, NE])
    din("ident_bf", [128, 128], BF16)
    din("ones1_f32", [1, 128]); din("ones128_bf", [128, 1], BF16)
    din("ones1_bf", [1, 128], BF16)
    din("DT", [128, HEADS_A * 2 * CHUNK], BF16)
    din("cs_bc", [128, HEADS_A * CHUNK], BF16); din("kv_bc", [128, HEADS_A * CHUNK], BF16)
    din("tri1", [128, 128], BF16); din("tri2", [128, 128], BF16)

    x_cur = nc.dram_tensor("x_cur", [S, D], F32, kind="Internal").ap()
    wqa_c = nc.dram_tensor("wqa_c", [NK, 128, NK * 128], BF16, kind="Internal").ap()
    wka_c = nc.dram_tensor("wka_c", [NK, 128, NK * 128], BF16, kind="Internal").ap()
    wva_c = nc.dram_tensor("wva_c", [4, 128, NK * 512], BF16, kind="Internal").ap()
    wgu_c = nc.dram_tensor("wgu_c", [NE, 8, 128, NK * 256], BF16, kind="Internal").ap()
    wqb_c = nc.dram_tensor("wqb_c", [HQ, 128, NK * 128], BF16, kind="Internal").ap()
    wkb_c = nc.dram_tensor("wkb_c", [4, 128, NK * 128], BF16, kind="Internal").ap()
    wvb_c = nc.dram_tensor("wvb_c", [8, 128, 2 * 512], BF16, kind="Internal").ap()
    wob_c = nc.dram_tensor("wob_c", [4, 128, NK * 512], BF16, kind="Internal").ap()
    wdn_c = nc.dram_tensor("wdn_c", [NE, 2, 128, 8 * 1024], BF16, kind="Internal").ap()
    dbg = {}
    if DBG_ATTNB:
        dbg["attnT"] = nc.dram_tensor("dbg_attnT", [128, NK * 1024], BF16, kind="ExternalOutput").ap()
        dbg["negm"] = nc.dram_tensor("dbg_negm", [1, 1024], BF16, kind="ExternalOutput").ap()
        dbg["kTb"] = nc.dram_tensor("dbg_kTb", [128, 4 * S], BF16, kind="ExternalOutput").ap()
        dbg["vb"] = nc.dram_tensor("dbg_vb", [128, NK * 512], BF16, kind="ExternalOutput").ap()
        dbg["qTh"] = nc.dram_tensor("dbg_qTh", [128, 1024], BF16, kind="ExternalOutput").ap()
    y_out = nc.dram_tensor("y", [S, D], F32, kind="ExternalOutput").ap()

    consts = host_constants()
    cdec = consts["chunk_decay"]

    es = ExitStack()
    with es:
        sch = Sched(nc, es)

        sb = lambda n, sh, dt=F32: es.enter_context(nc.sbuf_tensor(n, sh, dt))
        ident_bf = sb("identbf_sb", [128, 128], BF16)
        ones1 = sb("ones1_sb", [1, 128], F32)
        ones1b = sb("ones1b_sb", [1, 128], BF16)
        ones128 = sb("ones128_sb", [128, 1], BF16)
        gwT_sb = sb("gwT_sb", [128, NK * NE], BF16)
        bnst = sb("bnst_sb", [128, 24], F32)
        bnst2 = sb("bnst2_sb", [128, 24], F32)
        bnst3 = sb("bnst3_sb", [128, 8 * 24], F32)
        mvst = sb("mv_sb", [128, 2 * NK], F32)
        sd = sb("sd_sb", [128, NK], F32); rstd = sb("rstd_sb", [128, NK], F32)
        nb = sb("nb_sb", [128, NK], F32)
        xt = [sb(f"xt{i}", [128, D], F32) for i in range(2)]
        lntm = [sb(f"lntm{i}", [128, D], BF16) for i in range(2)]

        ps = lambda n, sh, dt=F32: es.enter_context(nc.psum_tensor(n, sh, dt))
        psf = [ps(f"psf{i}", [128, 512], F32) for i in range(4)]
        psb2 = ps("psb2", [128, 1024], F32)
        psbf = [ps(f"psbf{i}", [128, 512], BF16) for i in range(1)]
        pssm = ps("pssm", [1, 512], F32)

        cl = []
        for name, t in (("ident_bf", ident_bf), ("ones1_bf", ones1b),
                        ("ones1_f32", ones1), ("ones128_bf", ones128)):
            cl.append(sch.dma("sync", t[:], dr[name]))
        ev_gw = sch.dma("gpsimd", gwT_sb.ap().rearrange("p (k e) -> p k e", k=NK),
                        dr["gwT"])
        ev_consts = cl[-1]
        for eng in ("tensor", "vector", "scalar"):
            sch.wait(eng, "qX", ev_consts[1])

        last_x_store = None
        xt_last = {0: [], 1: []}
        stats_ready = {}
        psum_free = {id(p): None for p in psf + [psb2] + psbf + [pssm]}

        def pswait(eng, p):
            d = psum_free[id(p)]
            if d is not None:
                sch.waits(eng, [d])

        def psfree(p, mark):
            psum_free[id(p)] = mark

        def emit_ln(src, gis, lnT, lnT_stride, toff_fn, dep0=None, stats_marks=None):
            n = len(gis)
            sch.wait("vector", "SE", sch.cnt["SE"])
            sch.wait("scalar", "VE", sch.cnt["VE"])
            war = {}
            last_aggr = None
            for t, gi in enumerate(gis) if stats_marks is None else ():
                deps = [dep0] if dep0 else []
                if t >= 2:
                    deps += war[t - 2]
                else:
                    deps += xt_last[t % 2]
                ev = sch.dma("sync", xt[t % 2][:], src[gi * 128:(gi + 1) * 128, :], deps)
                sch.wait("vector", "qX", ev[1])
                mb = None
                for i4 in range(4):
                    mb = sch.run("vector", lambda e, i4=i4, b=xt[t % 2]:
                                 e.bn_stats(bnst[:, i4 * 6:(i4 + 1) * 6],
                                            b[:, i4 * 512:(i4 + 1) * 512]),
                                 inc="VE" if i4 == 3 else None)
                sch.wait("vector", "VE", mb[1])
                ma = sch.run("vector", lambda e, gi=gi:
                             e.bn_aggr(mvst[:, 2 * gi:2 * gi + 2], bnst[:, 0:24]), inc="VE")
                war[t] = [ma]
                xt_last[t % 2] = [ma]
                last_aggr = ma
            c0 = gis[0]
            mvv = mvst.ap().rearrange("p (g c) -> p g c", c=2)
            mean_v = mvv[:, c0:c0 + n, 0:1]
            var_v = mvv[:, c0:c0 + n, 1:2]
            sl = lambda b: b[:, c0:c0 + n]
            if stats_marks is None:
                sch.wait("vector", "VE", last_aggr[1])
            else:
                sch.wait("vector", "VE", max(stats_marks[gi][1] for gi in gis))
            m4 = sch.run("vector", lambda e: e.tensor_scalar(sl(sd), var_v, 1e-5, None, ALU.add), inc="VE")
            sch.wait("scalar", "VE", m4[1])
            m5 = sch.run("scalar", lambda e: e.activation(sl(sd), sl(sd), AF.Sqrt), inc="SE")
            sch.wait("vector", "SE", m5[1])
            m6 = sch.run("vector", lambda e: e.reciprocal(sl(rstd), sl(sd)), inc="VE")
            sch.wait("vector", "VE", m6[1])
            m7 = sch.run("vector", lambda e: e.scalar_tensor_tensor(
                sl(nb), mean_v, -1.0, sl(rstd), ALU.mult, ALU.mult), inc="VE")
            sch.wait("scalar", "VE", m7[1])
            norm_marks = {}
            evac_marks = {}
            war2 = {}
            tr_last = [None, None]
            lv = lnT.ap().rearrange("p (k t) -> p k t", t=lnT_stride)
            last_ev = None
            for t, gi in enumerate(gis):
                if t >= 2:
                    deps = war2[t - 2]
                else:
                    deps = xt_last[t % 2]
                ev = sch.dma("sync", xt[t % 2][:], src[gi * 128:(gi + 1) * 128, :], deps)
                last_ev = ev
                sch.wait("scalar", "qX", ev[1])
                if tr_last[t % 2] is not None:
                    sch.wait("scalar", "PE", tr_last[t % 2][1])
                mn = sch.run("scalar", lambda e, o=lntm[t % 2], i=xt[t % 2], gi=gi:
                             e.activation(o[:], i[:], AF.Identity,
                                          bias=nb[:, gi:gi + 1], scale=rstd[:, gi:gi + 1]),
                             inc="SE")
                norm_marks[gi] = mn
                toff = toff_fn(gi)
                gmarks = []
                mt = None
                for gblk in range(4):
                    p = psbf[0]
                    sch.wait("tensor", "SE", mn[1])
                    pswait("tensor", p)
                    for j in range(4):
                        k = gblk * 4 + j
                        mt = sch.run("tensor", lambda e, p=p, j=j, k=k, src_t=lntm[t % 2]:
                                     e.transpose(p[:, j * 128:(j + 1) * 128],
                                                 src_t[:, k * 128:(k + 1) * 128], ident_bf[:]),
                                     inc="PE" if j == 3 else None)
                    sch.wait("vector", "PE", mt[1])
                    me = sch.run("vector", lambda e, p=p, gblk=gblk, toff=toff:
                                 e.tensor_copy(lv[:, gblk * 4:gblk * 4 + 4, toff:toff + 128],
                                               p.ap().rearrange("p (k t) -> p k t", t=128)),
                                 inc="VE")
                    psfree(p, me)
                    gmarks.append(me)
                tr_last[t % 2] = mt
                evac_marks[gi] = gmarks[-1]
                war2[t] = [norm_marks[gi], gmarks[-1], ev]
                xt_last[t % 2] = [norm_marks[gi]]
            return norm_marks, evac_marks, last_ev

        def emit_attn_a():
            nonlocal last_x_store
            ph = ExitStack()
            with ph:
                lnTq = ph.enter_context(nc.sbuf_tensor("lnTq", [128, NK * 512], BF16))
                qT = ph.enter_context(nc.sbuf_tensor("qTa", [128, NK * 512], BF16))
                kT = ph.enter_context(nc.sbuf_tensor("kTa", [128, NK * 512], BF16))
                vtm = ph.enter_context(nc.sbuf_tensor("vta", [128, 4 * D], BF16))
                S32 = ph.enter_context(nc.sbuf_tensor("S32", [128, HEADS_A * 512], F32))
                Sbf = ph.enter_context(nc.sbuf_tensor("Sbf", [128, HEADS_A * 512], BF16))
                wcb = [ph.enter_context(nc.sbuf_tensor(f"wcba{i}", [128, NK * 128], BF16))
                       for i in range(3)]
                wvcb = [ph.enter_context(nc.sbuf_tensor(f"wvcba{i}", [128, NK * 512], BF16))
                        for i in range(2)]
                xc = [ph.enter_context(nc.sbuf_tensor(f"xca{i}", [128, D], F32)) for i in range(2)]
                qcs = [ph.enter_context(nc.sbuf_tensor(f"qcsa{i}", [128, 512], BF16)) for i in range(2)]
                DT_sb = ph.enter_context(nc.sbuf_tensor("DT_sb", [128, HEADS_A * 2 * CHUNK], BF16))
                cs_sb = ph.enter_context(nc.sbuf_tensor("cs_sb", [128, HEADS_A * CHUNK], BF16))
                kv_sb = ph.enter_context(nc.sbuf_tensor("kv_sb", [128, HEADS_A * CHUNK], BF16))
                bqa_sb = ph.enter_context(nc.sbuf_tensor("bqa_sb", [128, NK], F32))
                bka_sb = ph.enter_context(nc.sbuf_tensor("bka_sb", [128, NK], F32))
                bva_sb = ph.enter_context(nc.sbuf_tensor("bva_sb", [1, D], BF16))
                bva_bc = ph.enter_context(nc.sbuf_tensor("bva_bc", [128, D], BF16))
                pc = []
                for name, t in (("DT", DT_sb), ("cs_bc", cs_sb), ("kv_bc", kv_sb),
                                ("bqa", bqa_sb), ("bka", bka_sb)):
                    pc.append(sch.dma("sync", t[:], dr[name]))
                pv = sch.dma("gpsimd", bva_sb[:], dr["bva"])
                for eng in ("tensor", "vector", "scalar"):
                    sch.wait(eng, "qX", pc[-1][1])
                sch.wait("tensor", "qW", pv[1])
                for jq in range(4):
                    p = psf[jq % 2]
                    pswait("tensor", p)
                    mmb = sch.run("tensor", lambda e, p=p, jq=jq:
                                  e.matmul(p[:], ones1b[:], bva_sb[:, jq * 512:(jq + 1) * 512],
                                           start=True, stop=True), inc="PE")
                    sch.wait("vector", "PE", mmb[1])
                    meb = sch.run("vector", lambda e, p=p, jq=jq:
                                  e.tensor_copy(bva_bc[:, jq * 512:(jq + 1) * 512], p[:]), inc="VE")
                    psfree(p, meb)
                bva_bc_done = ("VE", sch.cnt["VE"])
                sch.wait("vector", "VE", bva_bc_done[1])
                ksc = [ph.enter_context(nc.sbuf_tensor(f"ksca{i}", [128, 512], BF16)) for i in range(2)]
                ktr = [ph.enter_context(nc.sbuf_tensor(f"ktra{i}", [128, 512], BF16)) for i in range(2)]
                atc = [ph.enter_context(nc.sbuf_tensor(f"atca{i}", [128, 512], BF16)) for i in range(2)]

                mS0 = sch.run("vector", lambda e: e.memset(S32[:], 0.0), inc="VE")
                S_copy_mark = {h: mS0 for h in range(HEADS_A)}
                S_fma_mark = {h: mS0 for h in range(HEADS_A)}
                psc2 = [psf[0], psb2]
                pso2 = [psf[1], psb2]
                psS2 = [psf[2], psf[3]]
                prev_out_mm = [None, None]
                prev_ktr_mm = [None, None]
                prev_S_mm = [None, None]
                prev_at_rd = [None, None]
                prev_store = [None, None]
                chunk_xwar = [None, None]
                wq_last_use = [[], [], []]
                wv_last_use = [[], []]
                last_proj_mm = None

                def do_ln(qt):
                    gis = list(range(qt * 4, qt * 4 + 4))
                    _, evac, _ = emit_ln(x_in, gis, lnTq, 512,
                                         lambda gi, qt=qt: gi * 128 - qt * 512,
                                         dep0=ev_consts if qt == 0 else None)
                    return evac[gis[-1]]

                ln_next = do_ln(0)
                for qt in range(4):
                    ln_done = ln_next

                    ldix = 0
                    for mat, (wdr, wcache, wsb, bias) in enumerate(
                            (("Wqa", wqa_c, qT, bqa_sb), ("Wka", wka_c, kT, bka_sb))):
                        for jb in range(NK):
                            slot = ldix % 3
                            if qt == 0:
                                ev = sch.dma("gpsimd",
                                             wcb[slot].ap().rearrange("p (k j) -> p k j", k=NK),
                                             dr[wdr][jb].rearrange("k p j -> p k j"),
                                             deps=wq_last_use[slot])
                                sev = sch.dma("sync", wcache[jb], wcb[slot][:], deps=[ev])
                                wsem = "qW"
                            else:
                                qq = "scalar" if slot % 2 == 0 else "sync"
                                ev = sch.dma(qq, wcb[slot][:], wcache[jb],
                                             deps=wq_last_use[slot])
                                sev = None
                                wsem = sch.QSEM[qq]
                            p = psf[ldix % 3]
                            sch.wait("tensor", wsem, ev[1])
                            sch.waits("tensor", [ln_done])
                            pswait("tensor", p)
                            mm = None
                            for k in range(NK):
                                mm = sch.run("tensor", lambda e, p=p, k=k, slot=slot:
                                             e.matmul(p[:], wcb[slot][:, k * 128:(k + 1) * 128],
                                                      lnTq[:, k * 512:(k + 1) * 512],
                                                      start=(k == 0), stop=(k == NK - 1)),
                                             inc="PE" if k == NK - 1 else None)
                            wq_last_use[slot] = [mm] + ([sev] if sev else [])
                            last_proj_mm = mm
                            sch.wait("scalar", "PE", mm[1])
                            me = sch.run("scalar", lambda e, p=p, wsb=wsb, jb=jb, bias=bias:
                                         e.activation(wsb[:, jb * 512:(jb + 1) * 512], p[:],
                                                      AF.Identity, bias=bias[:, jb:jb + 1]),
                                         inc="SE")
                            psfree(p, me)
                            ldix += 1
                    qk_evac_done = ("SE", sch.cnt["SE"])

                    for jb4 in range(4):
                        slot = jb4 % 2
                        if qt == 0:
                            ev = sch.dma("gpsimd",
                                         wvcb[slot].ap().rearrange("p (k j) -> p k j", k=NK),
                                         dr["Wva"][jb4].rearrange("k p j -> p k j"),
                                         deps=wv_last_use[slot])
                            vsev = sch.dma("sync", wva_c[jb4], wvcb[slot][:], deps=[ev])
                            vwsem = "qW"
                        else:
                            qq = "scalar" if jb4 % 2 == 0 else "sync"
                            ev = sch.dma(qq, wvcb[slot][:], wva_c[jb4],
                                         deps=wv_last_use[slot])
                            vsev = None
                            vwsem = sch.QSEM[qq]
                        for tt in range(4):
                            p = psf[tt % 3] if tt < 3 else psb2
                            pap = p[:] if tt < 3 else p[:, 0:512]
                            sch.wait("tensor", vwsem, ev[1])
                            sch.waits("tensor", [ln_done])
                            pswait("tensor", p)
                            mm = None
                            for k in range(NK):
                                mm = sch.run("tensor", lambda e, pap=pap, k=k, tt=tt, slot=slot:
                                             e.matmul(pap, lnTq[:, k * 512 + tt * 128: k * 512 + tt * 128 + 128],
                                                      wvcb[slot][:, k * 512:(k + 1) * 512],
                                                      start=(k == 0), stop=(k == NK - 1)),
                                             inc="PE" if k == NK - 1 else None)
                            wv_last_use[slot] = [mm] + ([vsev] if vsev else [])
                            last_proj_mm = mm
                            sch.wait("vector", "PE", mm[1])
                            me = sch.run("vector", lambda e, pap=pap, tt=tt, jb4=jb4:
                                         e.tensor_tensor(vtm[:, tt * D + jb4 * 512: tt * D + (jb4 + 1) * 512],
                                                         pap, bva_bc[:, jb4 * 512:(jb4 + 1) * 512], ALU.add),
                                         inc="VE")
                            psfree(p, me)
                    v_evac_done = ("VE", sch.cnt["VE"])

                    if qt + 1 < 4:
                        ln_next = do_ln(qt + 1)


                    for c in range(2):
                        n = qt * 2 + c
                        xev = []
                        for it in range(2):
                            gi = qt * 4 + c * 2 + it
                            deps = [prev_store[it], chunk_xwar[it]]
                            xev.append(sch.dma("sync", xc[it][:],
                                               x_in[gi * 128:(gi + 1) * 128, :], deps))
                        out_evac_marks = []
                        for h in range(HEADS_A):
                            u = (n * HEADS_A + h) % 2
                            qb = qcs[u]; kb = ksc[u]; tb_ = ktr[u]; ab = atc[u]
                            p_sc = psc2[u]
                            p_o = pso2[u]
                            psS = psS2[u]
                            oof = 0 if u == 0 else 512
                            p_kt = psbf[0]
                            tc = c * 256
                            sch.waits("vector", [qk_evac_done, prev_out_mm[u]])
                            mq = None
                            for dt_ in range(2):
                                mq = sch.run("vector", lambda e, dt_=dt_, h=h, tc=tc, qb=qb:
                                             e.tensor_tensor(qb[:, dt_ * 256:(dt_ + 1) * 256],
                                                             qT[:, (2 * h + dt_) * 512 + tc:(2 * h + dt_) * 512 + tc + 256],
                                                             cs_sb[:, h * 256:(h + 1) * 256], ALU.mult),
                                             inc="VE")
                            sch.waits("vector", [prev_ktr_mm[u]])
                            mk = None
                            for dt_ in range(2):
                                mk = sch.run("vector", lambda e, dt_=dt_, h=h, tc=tc, kb=kb:
                                             e.tensor_tensor(kb[:, dt_ * 256:(dt_ + 1) * 256],
                                                             kT[:, (2 * h + dt_) * 512 + tc:(2 * h + dt_) * 512 + tc + 256],
                                                             kv_sb[:, h * 256:(h + 1) * 256], ALU.mult),
                                             inc="VE")
                            sch.waits("tensor", [qk_evac_done])
                            pswait("tensor", p_sc)
                            msc = None
                            for jt in range(2):
                                for dt_ in range(2):
                                    msc = sch.run("tensor", lambda e, jt=jt, dt_=dt_, h=h, tc=tc, p_sc=p_sc:
                                                  e.matmul(p_sc[:, jt * 256:(jt + 1) * 256],
                                                           kT[:, (2 * h + dt_) * 512 + tc + jt * 128:
                                                              (2 * h + dt_) * 512 + tc + jt * 128 + 128],
                                                           qT[:, (2 * h + dt_) * 512 + tc:(2 * h + dt_) * 512 + tc + 256],
                                                           start=(dt_ == 0), stop=(dt_ == 1)),
                                                  inc="PE" if (jt == 1 and dt_ == 1) else None)
                            sch.wait("tensor", "VE", mk[1])
                            pswait("tensor", p_kt)
                            mtr = None
                            for jt in range(2):
                                for dt_ in range(2):
                                    mtr = sch.run("tensor", lambda e, jt=jt, dt_=dt_, kb=kb, p_kt=p_kt:
                                                  e.transpose(p_kt[:, (jt * 2 + dt_) * 128:(jt * 2 + dt_ + 1) * 128],
                                                              kb[:, dt_ * 256 + jt * 128: dt_ * 256 + jt * 128 + 128],
                                                              ident_bf[:]),
                                                  inc="PE" if (jt == 1 and dt_ == 1) else None)
                            prev_ktr_mm[u] = mtr
                            sch.wait("vector", "PE", msc[1])
                            sch.waits("vector", [prev_at_rd[u]])
                            mdt = None
                            for jt in range(2):
                                mdt = sch.run("vector", lambda e, jt=jt, h=h, ab=ab, p_sc=p_sc:
                                              e.tensor_tensor(ab[:, jt * 256:(jt + 1) * 256],
                                                              p_sc[:, jt * 256:(jt + 1) * 256],
                                                              DT_sb[:, (h * 2 + jt) * 256:(h * 2 + jt + 1) * 256],
                                                              ALU.mult),
                                              inc="VE")
                            psfree(p_sc, mdt)
                            sch.wait("vector", "PE", mtr[1])
                            sch.waits("vector", [prev_S_mm[u]])
                            mke = sch.run("vector", lambda e, tb_=tb_, p_kt=p_kt: e.tensor_copy(tb_[:], p_kt[:]), inc="VE")
                            psfree(p_kt, mke)
                            sch.wait("tensor", "VE", mdt[1])
                            sch.waits("tensor", [v_evac_done, ("VE", mq[1]),
                                                 ("VE", S_copy_mark[h][1]) if n > 0 else None])
                            pswait("tensor", p_o)
                            mo = None
                            for it in range(2):
                                nmm = 4 if n > 0 else 2
                                ix = 0
                                for jt in range(2):
                                    mo = sch.run("tensor", lambda e, it=it, jt=jt, h=h, c=c, ab=ab, p_o=p_o, oof=oof,
                                                 st=(ix == 0), sp=(ix == nmm - 1):
                                                 e.matmul(p_o[:, oof + it * 256: oof + it * 256 + 256],
                                                          ab[:, jt * 256 + it * 128: jt * 256 + it * 128 + 128],
                                                          vtm[:, (c * 2 + jt) * D + h * 256:(c * 2 + jt) * D + (h + 1) * 256],
                                                          start=st, stop=sp),
                                                 inc="PE" if (ix == nmm - 1) else None)
                                    ix += 1
                                if n > 0:
                                    for dt_ in range(2):
                                        mo = sch.run("tensor", lambda e, it=it, dt_=dt_, h=h, qb=qb, p_o=p_o, oof=oof,
                                                     sp=(ix == nmm - 1):
                                                     e.matmul(p_o[:, oof + it * 256: oof + it * 256 + 256],
                                                              qb[:, dt_ * 256 + it * 128: dt_ * 256 + it * 128 + 128],
                                                              Sbf[:, h * 512 + dt_ * 256: h * 512 + (dt_ + 1) * 256],
                                                              start=False, stop=sp),
                                                     inc="PE" if (ix == nmm - 1) else None)
                                        ix += 1
                            prev_out_mm[u] = mo
                            prev_at_rd[u] = mo
                            sch.wait("tensor", "VE", mke[1])
                            sch.waits("tensor", [("VE", S_fma_mark[h][1])])
                            pswait("tensor", psS)
                            ms = None
                            for dt_ in range(2):
                                for jt in range(2):
                                    ms = sch.run("tensor", lambda e, dt_=dt_, jt=jt, h=h, c=c, tb_=tb_:
                                                 e.matmul(psS[:, dt_ * 256:(dt_ + 1) * 256],
                                                          tb_[:, (jt * 2 + dt_) * 128:(jt * 2 + dt_ + 1) * 128],
                                                          vtm[:, (c * 2 + jt) * D + h * 256:(c * 2 + jt) * D + (h + 1) * 256],
                                                          start=(jt == 0), stop=(jt == 1)),
                                                 inc="PE" if (dt_ == 1 and jt == 1) else None)
                            prev_S_mm[u] = ms
                            sch.wait("vector", "PE", mo[1])
                            sch.waits("vector", [("qX", xev[0][1]), ("qX", xev[1][1])])
                            moe = None
                            for it in range(2):
                                moe = sch.run("vector", lambda e, it=it, h=h, p_o=p_o, oof=oof:
                                              e.tensor_tensor(xc[it][:, h * 256:(h + 1) * 256],
                                                              xc[it][:, h * 256:(h + 1) * 256],
                                                              p_o[:, oof + it * 256: oof + it * 256 + 256], ALU.add),
                                              inc="VE")
                            psfree(p_o, moe)
                            out_evac_marks.append(moe)
                            sch.wait("vector", "PE", ms[1])
                            msf = None
                            for dt_ in range(2):
                                msf = sch.run("vector", lambda e, dt_=dt_, h=h:
                                              e.scalar_tensor_tensor(S32[:, h * 512 + dt_ * 256: h * 512 + (dt_ + 1) * 256],
                                                                     S32[:, h * 512 + dt_ * 256: h * 512 + (dt_ + 1) * 256],
                                                                     float(cdec[h]),
                                                                     psS[:, dt_ * 256:(dt_ + 1) * 256],
                                                                     ALU.mult, ALU.add),
                                              inc="VE")
                            psfree(psS, msf)
                            S_fma_mark[h] = msf
                            sch.wait("vector", "VE", msf[1])
                            msc2 = sch.run("vector", lambda e, h=h:
                                           e.tensor_copy(Sbf[:, h * 512:(h + 1) * 512],
                                                         S32[:, h * 512:(h + 1) * 512]),
                                           inc="VE")
                            S_copy_mark[h] = msc2
                        for it in range(2):
                            gi = qt * 4 + c * 2 + it
                            sch.wait("vector", "VE", out_evac_marks[-1][1])
                            mb = None
                            for i4 in range(4):
                                mb = sch.run("vector", lambda e, i4=i4, it=it:
                                             e.bn_stats(bnst2[:, i4 * 6:(i4 + 1) * 6],
                                                        xc[it][:, i4 * 512:(i4 + 1) * 512]),
                                             inc="VE" if i4 == 3 else None)
                            sch.wait("vector", "VE", mb[1])
                            mag = sch.run("vector", lambda e, gi=gi:
                                          e.bn_aggr(mvst[:, 2 * gi:2 * gi + 2], bnst2[:, 0:24]), inc="VE")
                            stats_ready[gi] = mag
                            deps = [("VE", out_evac_marks[-1][1])]
                            prev_store[it] = sch.dma("sync", x_cur[gi * 128:(gi + 1) * 128, :],
                                                     xc[it][:], deps)
                            chunk_xwar[it] = mag
                last_x_store = ("qX", sch.cnt["qX"])
            return

        def emit_ffn(call, src, dst):
            nonlocal last_x_store
            HS = S // 2
            ph = ExitStack()
            with ph:
                lnT2 = [ph.enter_context(nc.sbuf_tensor(f"lnTf{call}_{i}", [128, NK * HS], BF16))
                        for i in range(2)]
                acc = ph.enter_context(nc.sbuf_tensor(f"accf{call}", [128, 8 * D], BF16))
                comb = ph.enter_context(nc.sbuf_tensor(f"comb{call}", [128, NK * NE], F32))
                glog = ph.enter_context(nc.sbuf_tensor(f"glog{call}", [128, 2 * NE + 8], F32))
                wgu = [ph.enter_context(nc.sbuf_tensor(f"wgu{call}_{i}", [128, NK * 256], BF16))
                       for i in range(2)]
                wdn = [ph.enter_context(nc.sbuf_tensor(f"wdn{call}_{i}", [128, 8 * 1024], BF16))
                       for i in range(2)]
                intT = ph.enter_context(nc.sbuf_tensor(f"intT{call}", [128, 8 * HS], BF16))
                tsil = ph.enter_context(nc.sbuf_tensor(f"tsil{call}", [128, 1024], BF16))
                ytmp = ph.enter_context(nc.sbuf_tensor(f"ytmp{call}", [128, 1024], BF16))
                wgu_use = [[], []]
                wdn_use = [[], []]
                sch.wait("tensor", "qW", ev_gw[1])

                ln_done2 = []
                for th in range(2):
                    gis = list(range(th * 8, th * 8 + 8))
                    lnT = lnT2[th]
                    sm = stats_ready if (FUSE_STATS and all(gi in stats_ready for gi in gis)) else None
                    _, evacs, _ = emit_ln(src, gis, lnT, HS,
                                          lambda gi, th=th: gi * 128 - th * HS,
                                          stats_marks=sm)
                    ln_done2.append(evacs[gis[-1]])
                    for gi in gis:
                        gl = gi - th * 8
                        p = psf[gl % 2]
                        sch.waits("tensor", [evacs[gi]])
                        pswait("tensor", p)
                        mm = None
                        for k in range(NK):
                            mm = sch.run("tensor", lambda e, p=p, k=k, gl=gl, lnT=lnT:
                                         e.matmul(p[:, 0:NE],
                                                  lnT[:, k * HS + gl * 128: k * HS + gl * 128 + 128],
                                                  gwT_sb[:, k * NE:(k + 1) * NE],
                                                  start=(k == 0), stop=(k == NK - 1)),
                                         inc="PE" if k == NK - 1 else None)
                        lg = glog[:, 0:NE]; lg2 = glog[:, NE:2 * NE]
                        mmx = glog[:, 2 * NE:2 * NE + 2]; ssx = glog[:, 2 * NE + 2:2 * NE + 4]
                        den = glog[:, 2 * NE + 4:2 * NE + 5]; rr = glog[:, 2 * NE + 5:2 * NE + 6]
                        ww = glog[:, 2 * NE + 6:2 * NE + 8]
                        sch.wait("vector", "PE", mm[1])
                        mv = sch.run("vector", lambda e, p=p: e.tensor_copy(lg, p[:, 0:NE]), inc="VE")
                        psfree(p, mv)
                        sch.wait("vector", "VE", mv[1])
                        mv = sch.run("vector", lambda e: e.reduce_max(mmx[:, 0:1], lg, axis=AX.X), inc="VE")
                        sch.wait("vector", "VE", mv[1])
                        mv = sch.run("vector", lambda e: e.tensor_scalar(lg2, lg, mmx[:, 0:1], None, ALU.is_ge), inc="VE")
                        sch.wait("vector", "VE", mv[1])
                        mv = sch.run("vector", lambda e: e.scalar_tensor_tensor(lg2, lg2, -1e4, lg, ALU.mult, ALU.add), inc="VE")
                        sch.wait("vector", "VE", mv[1])
                        mv = sch.run("vector", lambda e: e.reduce_max(mmx[:, 1:2], lg2, axis=AX.X), inc="VE")
                        sch.wait("scalar", "VE", mv[1])
                        msig = sch.run("scalar", lambda e: e.activation(ssx, mmx, AF.Sigmoid), inc="SE")
                        sch.wait("vector", "SE", msig[1])
                        mv = sch.run("vector", lambda e: e.tensor_add(den, ssx[:, 0:1], ssx[:, 1:2]), inc="VE")
                        sch.wait("vector", "VE", mv[1])
                        mv = sch.run("vector", lambda e: e.reciprocal(rr, den), inc="VE")
                        sch.wait("vector", "VE", mv[1])
                        mv = sch.run("vector", lambda e: e.tensor_scalar(ww, ssx, rr, None, ALU.mult), inc="VE")
                        sch.wait("vector", "VE", mv[1])
                        mv = sch.run("vector", lambda e: e.tensor_scalar(lg, lg, mmx[:, 0:1], None, ALU.is_ge), inc="VE")
                        sch.wait("vector", "VE", mv[1])
                        mv = sch.run("vector", lambda e: e.tensor_scalar(lg2, lg2, mmx[:, 1:2], None, ALU.is_ge), inc="VE")
                        sch.wait("vector", "VE", mv[1])
                        mv = sch.run("vector", lambda e: e.tensor_scalar(lg, lg, ww[:, 0:1], None, ALU.mult), inc="VE")
                        sch.wait("vector", "VE", mv[1])
                        mv = sch.run("vector", lambda e, gi=gi: e.scalar_tensor_tensor(
                            comb[:, gi * NE:(gi + 1) * NE], lg2, ww[:, 1:2], lg, ALU.mult, ALU.add), inc="VE")
                gating_done = ("VE", sch.cnt["VE"])

                for th in range(2):
                    lnT = lnT2[th]
                    ln_done = ln_done2[th]
                    tsil_rd = [None, None]
                    ytmp_rd = [None, None]
                    sch.wait("scalar", "VE", sch.cnt["VE"])
                    first_pass = (call == 0 and th == 0)
                    for e_ix in range(NE):
                        guev = [None] * 8

                        def load_gu(mp, e_ix=e_ix):
                            slot = (th * 8 + e_ix * 8 + mp) % 2
                            dep = wgu_use[slot] if wgu_use[slot] else [ln_done2[1 if call == 0 else 0]]
                            if first_pass:
                                ev = sch.dma("gpsimd",
                                             wgu[slot].ap().rearrange("p (k j) -> p k j", k=NK),
                                             dr["Wgu"][e_ix, mp].rearrange("k p j -> p k j"),
                                             deps=dep)
                                sev = sch.dma("sync", wgu_c[e_ix, mp], wgu[slot][:], deps=[ev])
                                guev[mp] = (ev, "qW", sev)
                            else:
                                qq = "scalar" if slot == 0 else "sync"
                                ev = sch.dma(qq, wgu[slot][:], wgu_c[e_ix, mp], deps=dep)
                                guev[mp] = (ev, sch.QSEM[qq], None)

                        load_gu(0)
                        load_gu(1)
                        dnev = []
                        for hh in range(2):
                            slot = (th * 2 + e_ix * 2 + hh) % 2
                            dep = wdn_use[slot] if wdn_use[slot] else [ln_done2[1 if call == 0 else 0]]
                            if first_pass:
                                ev = sch.dma("gpsimd",
                                             wdn[slot].ap().rearrange("p (k j) -> p k j", k=8),
                                             dr["Wdn"][e_ix, hh].rearrange("k p j -> p k j"),
                                             deps=dep)
                                sev = sch.dma("sync", wdn_c[e_ix, hh], wdn[slot][:], deps=[ev])
                                dnev.append((ev, "qW", sev))
                            else:
                                ev = sch.dma("scalar", wdn[slot][:], wdn_c[e_ix, hh], deps=dep)
                                dnev.append((ev, "qS", None))
                        int_marks = []
                        for mp in range(8):
                            slot = (th * 8 + e_ix * 8 + mp) % 2
                            for tb in range(2):
                                pg = psf[0] if tb % 2 == 0 else psf[1]
                                pu = psf[2] if tb % 2 == 0 else psf[3]
                                puap = pu[:]
                                sch.wait("tensor", guev[mp][1], guev[mp][0][1])
                                pswait("tensor", pg); pswait("tensor", pu)
                                mmg = mmu = None
                                for k in range(NK):
                                    mmg = sch.run("tensor", lambda e, pg=pg, k=k, tb=tb, slot=slot, lnT=lnT:
                                                  e.matmul(pg[:], wgu[slot][:, k * 256:k * 256 + 128],
                                                           lnT[:, k * HS + tb * 512:k * HS + (tb + 1) * 512],
                                                           start=(k == 0), stop=(k == NK - 1)),
                                                  inc="PE" if k == NK - 1 else None)
                                    mmu = sch.run("tensor", lambda e, puap=puap, k=k, tb=tb, slot=slot, lnT=lnT:
                                                  e.matmul(puap, wgu[slot][:, k * 256 + 128:k * 256 + 256],
                                                           lnT[:, k * HS + tb * 512:k * HS + (tb + 1) * 512],
                                                           start=(k == 0), stop=(k == NK - 1)),
                                                  inc="PE" if k == NK - 1 else None)
                                if tb == 1:
                                    wgu_use[slot] = [mmu] + ([guev[mp][2]] if guev[mp][2] else [])
                                    if mp + 2 < 8:
                                        load_gu(mp + 2)
                                ts_slot = (mp * 2 + tb) % 2
                                sch.wait("scalar", "PE", mmg[1])
                                sch.waits("scalar", [tsil_rd[ts_slot]])
                                msl = sch.run("scalar", lambda e, pg=pg, sl=ts_slot:
                                              e.activation(tsil[:, sl * 512:(sl + 1) * 512], pg[:], AF.Silu),
                                              inc="SE")
                                psfree(pg, msl)
                                sch.wait("vector", "SE", msl[1])
                                sch.wait("vector", "PE", mmu[1])
                                mi = sch.run("vector", lambda e, puap=puap, mp=mp, tb=tb, sl=ts_slot:
                                             e.tensor_tensor(intT[:, mp * HS + tb * 512: mp * HS + (tb + 1) * 512],
                                                             tsil[:, sl * 512:(sl + 1) * 512], puap, ALU.mult),
                                             inc="VE")
                                tsil_rd[ts_slot] = mi
                                psfree(pu, mi)
                                int_marks.append(mi)
                        int_done = int_marks[-1]
                        for hh in range(2):
                            slot = (th * 2 + e_ix * 2 + hh) % 2
                            last_y = None
                            for tt in range(8):
                                for jb in range(2):
                                    p = psf[(tt * 2 + jb) % 4]
                                    sch.wait("tensor", dnev[hh][1], dnev[hh][0][1])
                                    sch.waits("tensor", [("VE", int_done[1])])
                                    pswait("tensor", p)
                                    mm = None
                                    for k in range(8):
                                        mm = sch.run("tensor", lambda e, p=p, k=k, tt=tt, jb=jb, slot=slot:
                                                     e.matmul(p[:], intT[:, k * HS + tt * 128: k * HS + tt * 128 + 128],
                                                              wdn[slot][:, k * 1024 + jb * 512: k * 1024 + (jb + 1) * 512],
                                                              start=(k == 0), stop=(k == 7)),
                                                     inc="PE" if k == 7 else None)
                                    last_y = mm
                                    sch.wait("scalar", "PE", mm[1])
                                    sch.waits("scalar", [gating_done])
                                    c_ap = comb[:, (th * 8 + tt) * NE + e_ix: (th * 8 + tt) * NE + e_ix + 1]
                                    if e_ix == 0:
                                        mev = sch.run("scalar", lambda e, p=p, tt=tt, hh=hh, jb=jb, c_ap=c_ap:
                                                      e.activation(acc[:, tt * D + hh * 1024 + jb * 512:
                                                                       tt * D + hh * 1024 + (jb + 1) * 512],
                                                                   p[:], AF.Identity, scale=c_ap),
                                                      inc="SE")
                                        psfree(p, mev)
                                    else:
                                        y_slot = (tt * 2 + jb) % 2
                                        sch.waits("scalar", [ytmp_rd[y_slot]])
                                        mev = sch.run("scalar", lambda e, p=p, c_ap=c_ap, sl=y_slot:
                                                      e.activation(ytmp[:, sl * 512:(sl + 1) * 512],
                                                                   p[:], AF.Identity, scale=c_ap),
                                                      inc="SE")
                                        psfree(p, mev)
                                        sch.wait("vector", "SE", mev[1])
                                        ma = sch.run("vector", lambda e, tt=tt, hh=hh, jb=jb, sl=y_slot:
                                                     e.tensor_tensor(acc[:, tt * D + hh * 1024 + jb * 512:
                                                                         tt * D + hh * 1024 + (jb + 1) * 512],
                                                                     acc[:, tt * D + hh * 1024 + jb * 512:
                                                                         tt * D + hh * 1024 + (jb + 1) * 512],
                                                                     ytmp[:, sl * 512:(sl + 1) * 512], ALU.add),
                                                     inc="VE")
                                        ytmp_rd[y_slot] = ma
                            wdn_use[slot] = [last_y] + ([dnev[hh][2]] if dnev[hh][2] else [])
                    acc_done_v = ("VE", sch.cnt["VE"])
                    acc_done_s = ("SE", sch.cnt["SE"])
                    prev = [None, None]
                    for tt in range(8):
                        gi = th * 8 + tt
                        deps = list(xt_last[tt % 2])
                        if prev[tt % 2] is not None:
                            deps.append(prev[tt % 2])
                        ev = sch.dma("sync", xt[tt % 2][:], src[gi * 128:(gi + 1) * 128, :], deps)
                        sch.wait("vector", "qX", ev[1])
                        sch.waits("vector", [acc_done_v, acc_done_s])
                        ma = sch.run("vector", lambda e, tt=tt:
                                     e.tensor_tensor(xt[tt % 2][:], xt[tt % 2][:],
                                                     acc[:, tt * D:(tt + 1) * D], ALU.add),
                                     inc="VE")
                        xt_last[tt % 2] = [ma]
                        if call == 0:
                            mb = None
                            for i4 in range(4):
                                mb = sch.run("vector", lambda e, i4=i4, tt=tt:
                                             e.bn_stats(bnst2[:, i4 * 6:(i4 + 1) * 6],
                                                        xt[tt % 2][:, i4 * 512:(i4 + 1) * 512]),
                                             inc="VE" if i4 == 3 else None)
                            sch.wait("vector", "VE", mb[1])
                            mag = sch.run("vector", lambda e, gi=gi:
                                          e.bn_aggr(mvst[:, 2 * gi:2 * gi + 2], bnst2[:, 0:24]), inc="VE")
                            stats_ready[gi] = mag
                            xt_last[tt % 2] = [mag]
                        prev[tt % 2] = sch.dma("sync", dst[gi * 128:(gi + 1) * 128, :], xt[tt % 2][:],
                                               deps=[ma])
                    last_x_store = ("qX", sch.cnt["qX"])

        def emit_attn_b(src, dst):
            nonlocal last_x_store
            HS = S // 2
            ph = ExitStack()
            with ph:
                attnT = ph.enter_context(nc.sbuf_tensor("attnTb", [128, NK * HS], BF16))
                lnT = ph.enter_context(nc.sbuf_tensor("lnTb", [128, NK * HS], BF16))
                kTb = ph.enter_context(nc.sbuf_tensor("kTb", [128, 4 * S], BF16))
                vb = ph.enter_context(nc.sbuf_tensor("vbb", [128, NK * 512], BF16))
                qTh = [ph.enter_context(nc.sbuf_tensor(f"qThb{i}", [128, HS], BF16)) for i in range(2)]
                wqcb = [ph.enter_context(nc.sbuf_tensor(f"wqcbb{i}", [128, NK * 128], BF16)) for i in range(2)]
                wkcb = [ph.enter_context(nc.sbuf_tensor(f"wkcbb{i}", [128, NK * 128], BF16)) for i in range(2)]
                wvch = [ph.enter_context(nc.sbuf_tensor(f"wvchb{i}", [128, 2 * 512], BF16)) for i in range(2)]
                wocb = [ph.enter_context(nc.sbuf_tensor(f"wocbb{i}", [128, NK * 512], BF16)) for i in range(2)]
                xcb = [ph.enter_context(nc.sbuf_tensor(f"xcb{i}", [128, 512], F32)) for i in range(2)]
                probsA = ph.enter_context(nc.sbuf_tensor("probsAb", [128, 8 * 640], BF16))
                maskA = ph.enter_context(nc.sbuf_tensor("maskAb", [128, 8 * 256], BF16))
                rbcA = ph.enter_context(nc.sbuf_tensor("rbcAb", [128, 8 * 128], F32))
                rrow4 = ph.enter_context(nc.sbuf_tensor("rrow4b", [1, 512], F32))
                tri1_sb = ph.enter_context(nc.sbuf_tensor("tri1_sb", [128, 128], BF16))
                tri2_sb = ph.enter_context(nc.sbuf_tensor("tri2_sb", [128, 128], BF16))
                pc = []
                for name, t in (("tri1", tri1_sb), ("tri2", tri2_sb)):
                    pc.append(sch.dma("sync", t[:], dr[name]))
                sch.wait("vector", "qX", pc[-1][1])

                wk_use = [[], []]
                wv_use = [[], []]
                wq_use = [[], []]
                wo_use = [[], []]
                rbc_war = None
                prev_xst = [None, None]
                for th in range(2):
                    gis = list(range(th * 8, th * 8 + 8))
                    sm = stats_ready if (FUSE_STATS and all(gi in stats_ready for gi in gis)) else None
                    _, evacs, _ = emit_ln(src, gis, lnT, HS,
                                          lambda gi, th=th: gi * 128 - th * HS,
                                          stats_marks=sm)
                    ln_done = evacs[gis[-1]]

                    for jb in range(4):
                        slot = jb % 2
                        dep = wk_use[slot] if wk_use[slot] else [ln_done]
                        if th == 0:
                            ev = sch.dma("gpsimd", wkcb[slot].ap().rearrange("p (k j) -> p k j", k=NK),
                                         dr["Wkb"][jb].rearrange("k p j -> p k j"), deps=dep)
                            ksev = sch.dma("sync", wkb_c[jb], wkcb[slot][:], deps=[ev])
                            kwsem = "qW"
                        else:
                            qq = "scalar" if slot == 0 else "sync"
                            ev = sch.dma(qq, wkcb[slot][:], wkb_c[jb], deps=dep)
                            ksev = None
                            kwsem = sch.QSEM[qq]
                        for tb in range(2):
                            p = psf[(jb * 2 + tb) % 3]
                            sch.waits("tensor", [ln_done, (kwsem, ev[1])])
                            pswait("tensor", p)
                            mm = None
                            for k in range(NK):
                                mm = sch.run("tensor", lambda e, p=p, k=k, tb=tb, slot=slot:
                                             e.matmul(p[:], wkcb[slot][:, k * 128:(k + 1) * 128],
                                                      lnT[:, k * HS + tb * 512: k * HS + (tb + 1) * 512],
                                                      start=(k == 0), stop=(k == NK - 1)),
                                             inc="PE" if k == NK - 1 else None)
                            wk_use[slot] = [mm] + ([ksev] if ksev else [])
                            sch.wait("vector", "PE", mm[1])
                            me = sch.run("vector", lambda e, p=p, jb=jb, tb=tb, th=th:
                                         e.tensor_copy(kTb[:, jb * S + th * HS + tb * 512:
                                                           jb * S + th * HS + (tb + 1) * 512], p[:]),
                                         inc="VE")
                            psfree(p, me)
                    for ttg in range(2):
                        mmq = [None] * 4
                        for k2 in range(8):
                            slot = k2 % 2
                            dep = wv_use[slot] if wv_use[slot] else [ln_done]
                            if th == 0 and ttg == 0:
                                ev = sch.dma("gpsimd", wvch[slot].ap().rearrange("p (k c) -> p k c", k=2),
                                             dr["Wvb"][2 * k2:2 * k2 + 2].rearrange("k p c -> p k c"),
                                             deps=dep)
                                vsev = sch.dma("sync", wvb_c[k2], wvch[slot][:], deps=[ev])
                                vwsem = "qW"
                            else:
                                qq = "scalar" if slot == 0 else "sync"
                                ev = sch.dma(qq, wvch[slot][:], wvb_c[k2], deps=dep)
                                vsev = None
                                vwsem = sch.QSEM[qq]
                            sch.waits("tensor", [ln_done, (vwsem, ev[1])])
                            for ti in range(4):
                                ttl = ttg * 4 + ti
                                p = psf[ti] if ti < 3 else psb2
                                pap = p[:] if ti < 3 else p[:, 0:512]
                                if k2 == 0:
                                    pswait("tensor", p)
                                mm = None
                                for kk in range(2):
                                    k = 2 * k2 + kk
                                    mark = (k == NK - 1) or (ti == 3 and kk == 1)
                                    mm = sch.run("tensor", lambda e, pap=pap, k=k, kk=kk, ttl=ttl, slot=slot:
                                                 e.matmul(pap, lnT[:, k * HS + ttl * 128: k * HS + ttl * 128 + 128],
                                                          wvch[slot][:, kk * 512:(kk + 1) * 512],
                                                          start=(k == 0), stop=(k == NK - 1)),
                                                 inc="PE" if mark else None)
                                mmq[ti] = mm
                            wv_use[slot] = [mmq[3]] + ([vsev] if vsev else [])
                        for ti in range(4):
                            ttl = ttg * 4 + ti
                            tt = th * 8 + ttl
                            p = psf[ti] if ti < 3 else psb2
                            pap = p[:] if ti < 3 else p[:, 0:512]
                            sch.wait("vector", "PE", mmq[ti][1])
                            me = sch.run("vector", lambda e, pap=pap, tt=tt:
                                         e.tensor_copy(vb[:, tt * 512:(tt + 1) * 512], pap), inc="VE")
                            psfree(p, me)
                    kv_done = ("VE", sch.cnt["VE"])

                    def emit_qproj(h):
                        wslot = h % 2
                        dep = wq_use[wslot] if wq_use[wslot] else [ln_done]
                        if th == 0:
                            ev = sch.dma("gpsimd", wqcb[wslot].ap().rearrange("p (k j) -> p k j", k=NK),
                                         dr["Wqb"][h].rearrange("k p j -> p k j"), deps=dep)
                            qsev = sch.dma("sync", wqb_c[h], wqcb[wslot][:], deps=[ev])
                            qwsem = "qW"
                        else:
                            qq = "scalar" if wslot == 0 else "sync"
                            ev = sch.dma(qq, wqcb[wslot][:], wqb_c[h], deps=dep)
                            qsev = None
                            qwsem = sch.QSEM[qq]
                        for tb in range(2):
                            p = psf[2]
                            sch.waits("tensor", [ln_done, (qwsem, ev[1])])
                            pswait("tensor", p)
                            mm = None
                            for k in range(NK):
                                mm = sch.run("tensor", lambda e, p=p, k=k, tb=tb, wslot=wslot:
                                             e.matmul(p[:], wqcb[wslot][:, k * 128:(k + 1) * 128],
                                                      lnT[:, k * HS + tb * 512: k * HS + (tb + 1) * 512],
                                                      start=(k == 0), stop=(k == NK - 1)),
                                             inc="PE" if k == NK - 1 else None)
                            wq_use[wslot] = [mm] + ([qsev] if qsev else [])
                            sch.wait("vector", "PE", mm[1])
                            me = sch.run("vector", lambda e, tb=tb, wslot=wslot:
                                         e.tensor_copy(qTh[wslot][:, tb * 512:(tb + 1) * 512], p[:]),
                                         inc="VE")
                            psfree(psf[2], me)
                        return ("VE", sch.cnt["VE"])

                    qready = emit_qproj(0)
                    for h in range(HQ):
                        kvh = h // 4
                        hs = h % 2
                        qT_h = qTh[hs]

                        def kts_of(ql):
                            qt = th * 8 + ql
                            return [kt for kt in range(qt - 4, qt + 1) if kt >= 0]

                        mask_done = [None] * 8
                        for ql in range(8):
                            kts = kts_of(ql)
                            nsl = len(kts)
                            has2 = nsl == 5
                            if ql % 2 == 0:
                                regions = [(psb2, 0, 0, min(nsl, 8))]
                            else:
                                regions = [(psf[0], 0, 0, min(nsl, 4))]
                                if nsl > 4:
                                    regions.append((psf[1], 0, 4, nsl - 4))
                            sch.waits("tensor", [qready, kv_done])
                            mmlast = None
                            for (pt, base, lo, ns) in regions:
                                pswait("tensor", pt)
                                for s_ix in range(lo, lo + ns):
                                    kt = kts[s_ix]
                                    col = base + (s_ix - lo) * 128
                                    mmlast = sch.run("tensor", lambda e, pt=pt, col=col, kt=kt, ql=ql, kvh=kvh, qT_h=qT_h:
                                            e.matmul(pt[:, col:col + 128],
                                                     kTb[:, kvh * S + kt * 128: kvh * S + (kt + 1) * 128],
                                                     qT_h[:, ql * 128:(ql + 1) * 128],
                                                     start=True, stop=True),
                                            inc="PE")
                            sch.wait("scalar", "PE", mmlast[1])
                            mex = None
                            dst_of = ql * 640
                            for (pt, base, lo, ns) in regions:
                                mex = sch.run("scalar", lambda e, pt=pt, base=base, lo=lo, ns=ns, dst_of=dst_of:
                                              e.activation(probsA[:, dst_of + lo * 128: dst_of + (lo + ns) * 128],
                                                           pt[:, base:base + ns * 128],
                                                           AF.Exp, scale=1.0 / 128.0),
                                              inc="SE")
                                psfree(pt, mex)
                            sch.wait("vector", "SE", mex[1])
                            mm1 = sch.run("vector", lambda e, ql=ql, nsl=nsl:
                                          e.tensor_tensor(maskA[:, ql * 256:ql * 256 + 128],
                                                          probsA[:, ql * 640 + (nsl - 1) * 128: ql * 640 + nsl * 128],
                                                          tri1_sb[:], ALU.mult), inc="VE")
                            if has2:
                                mm1 = sch.run("vector", lambda e, ql=ql:
                                              e.tensor_tensor(maskA[:, ql * 256 + 128:ql * 256 + 256],
                                                              probsA[:, ql * 640: ql * 640 + 128],
                                                              tri2_sb[:], ALU.mult), inc="VE")
                            mask_done[ql] = mm1

                        def slot_ap(ql, s_ix):
                            kts = kts_of(ql)
                            nsl = len(kts)
                            has2 = nsl == 5
                            if s_ix == nsl - 1:
                                return maskA[:, ql * 256:ql * 256 + 128]
                            if has2 and s_ix == 0:
                                return maskA[:, ql * 256 + 128:ql * 256 + 256]
                            return probsA[:, ql * 640 + s_ix * 128: ql * 640 + (s_ix + 1) * 128]

                        if h + 1 < HQ:
                            qready = emit_qproj(h + 1)

                        for r4 in range(2):
                            msum = None
                            pswait("tensor", pssm)
                            for qi in range(4):
                                ql = r4 * 4 + qi
                                kts = kts_of(ql)
                                nsl = len(kts)
                                sch.waits("tensor", [("VE", mask_done[ql][1])])
                                for s_ix in range(nsl):
                                    sp_ = s_ix == nsl - 1
                                    msum = sch.run("tensor", lambda e, qi=qi, ap=slot_ap(ql, s_ix),
                                                   st=(s_ix == 0), sp=sp_:
                                                   e.matmul(pssm[0:1, qi * 128:(qi + 1) * 128], ones128[:], ap,
                                                            start=st, stop=sp),
                                                   inc="PE" if sp_ else None)
                            sch.wait("vector", "PE", msum[1])
                            mrc = sch.run("vector", lambda e: e.reciprocal(rrow4[:], pssm[0:1, 0:512]), inc="VE")
                            psfree(pssm, mrc)
                            p_rb = psf[0]
                            sch.wait("tensor", "VE", mrc[1])
                            pswait("tensor", p_rb)
                            mmr = sch.run("tensor", lambda e, p_rb=p_rb:
                                          e.matmul(p_rb[:], ones1[:], rrow4[0:1, 0:512],
                                                   start=True, stop=True),
                                          inc="PE")
                            sch.wait("vector", "PE", mmr[1])
                            mrb = sch.run("vector", lambda e, r4=r4, p_rb=p_rb:
                                          e.tensor_copy(rbcA[:, r4 * 512:(r4 + 1) * 512], p_rb[:]), inc="VE")
                            psfree(p_rb, mrb)
                        rbc_done = ("VE", sch.cnt["VE"])

                        for r4 in range(2):
                            p_o = psf[2 + r4]
                            pswait("tensor", p_o)
                            mmo = None
                            for qi in range(4):
                                ql = r4 * 4 + qi
                                kts = kts_of(ql)
                                nsl = len(kts)
                                sch.waits("tensor", [("VE", mask_done[ql][1]), kv_done])
                                for s_ix in range(nsl):
                                    kt = kts[s_ix]
                                    sp_ = s_ix == nsl - 1
                                    mmo = sch.run("tensor", lambda e, qi=qi, kt=kt, kvh=kvh, ap=slot_ap(ql, s_ix),
                                                  st=(s_ix == 0), sp=sp_, p_o=p_o:
                                                  e.matmul(p_o[:, qi * 128:(qi + 1) * 128],
                                                           vb[:, kt * 512 + kvh * 128: kt * 512 + (kvh + 1) * 128],
                                                           ap, start=st, stop=sp),
                                                  inc="PE" if sp_ else None)
                            sch.wait("vector", "PE", mmo[1])
                            sch.waits("vector", [rbc_done])
                            moe = sch.run("vector", lambda e, h=h, r4=r4, p_o=p_o:
                                          e.tensor_tensor(attnT[:, h * HS + r4 * 512: h * HS + (r4 + 1) * 512],
                                                          p_o[:], rbcA[:, r4 * 512:(r4 + 1) * 512], ALU.mult),
                                          inc="VE")
                            psfree(p_o, moe)
                            rbc_war = moe
                    attnT_done = ("VE", sch.cnt["VE"])
                    if DBG_ATTNB and th == 0:
                        sch.dma("sync", dbg["attnT"], attnT[:], deps=[attnT_done])
                        sch.dma("sync", dbg["negm"], negm[:], deps=[attnT_done])
                        sch.dma("sync", dbg["kTb"], kTb[:], deps=[attnT_done])
                        sch.dma("sync", dbg["vb"], vb[:], deps=[attnT_done])
                        sch.dma("sync", dbg["qTh"], qTh[:], deps=[attnT_done])

                    for jb in range(4):
                        slot = jb % 2
                        dep = wo_use[slot] if wo_use[slot] else [attnT_done]
                        if th == 0:
                            ev = sch.dma("gpsimd", wocb[slot].ap().rearrange("p (k j) -> p k j", k=NK),
                                         dr["Wob"][jb].rearrange("k p j -> p k j"), deps=dep)
                            osev = sch.dma("sync", wob_c[jb], wocb[slot][:], deps=[ev])
                            owsem = "qW"
                        else:
                            qq = "scalar" if slot == 0 else "sync"
                            ev = sch.dma(qq, wocb[slot][:], wob_c[jb], deps=dep)
                            osev = None
                            owsem = sch.QSEM[qq]
                        for ttl in range(8):
                            gi = th * 8 + ttl
                            bslot = ttl % 2
                            xev = sch.dma("sync", xcb[bslot][:, 0:512],
                                          src[gi * 128:(gi + 1) * 128, jb * 512:(jb + 1) * 512],
                                          deps=[prev_xst[bslot] if prev_xst[bslot] is not None else attnT_done])
                            p = psf[(jb * 8 + ttl) % 3]
                            sch.waits("tensor", [attnT_done, (owsem, ev[1])])
                            pswait("tensor", p)
                            mm = None
                            for k in range(NK):
                                mm = sch.run("tensor", lambda e, p=p, k=k, ttl=ttl, slot=slot:
                                             e.matmul(p[:], attnT[:, k * HS + ttl * 128: k * HS + ttl * 128 + 128],
                                                      wocb[slot][:, k * 512:(k + 1) * 512],
                                                      start=(k == 0), stop=(k == NK - 1)),
                                             inc="PE" if k == NK - 1 else None)
                            wo_use[slot] = [mm] + ([osev] if osev else [])
                            sch.wait("vector", "PE", mm[1])
                            sch.wait("vector", "qX", xev[1])
                            ma = sch.run("vector", lambda e, p=p, bslot=bslot:
                                         e.tensor_tensor(xcb[bslot][:, 0:512], xcb[bslot][:, 0:512],
                                                         p[:], ALU.add),
                                         inc="VE")
                            psfree(p, ma)
                            sch.wait("vector", "VE", ma[1])
                            mb = sch.run("vector", lambda e, ttl=ttl, jb=jb, bslot=bslot:
                                         e.bn_stats(bnst3[:, ttl * 24 + jb * 6: ttl * 24 + (jb + 1) * 6],
                                                    xcb[bslot][:, 0:512]),
                                         inc="VE")
                            if jb == 3:
                                sch.wait("vector", "VE", mb[1])
                                mag = sch.run("vector", lambda e, gi=gi, ttl=ttl:
                                              e.bn_aggr(mvst[:, 2 * gi:2 * gi + 2],
                                                        bnst3[:, ttl * 24:(ttl + 1) * 24]), inc="VE")
                                stats_ready[gi] = mag
                            prev_xst[bslot] = sch.dma("sync", dst[gi * 128:(gi + 1) * 128, jb * 512:(jb + 1) * 512],
                                                      xcb[bslot][:, 0:512], deps=[ma, mb])
            last_x_store = ("qX", sch.cnt["qX"])

        srcs = [x_in, x_cur, x_cur, x_cur]
        if nsub >= 1:
            emit_attn_a()
        if nsub >= 2:
            emit_ffn(0, x_cur if nsub >= 2 else x_in, x_cur)
        if nsub >= 3:
            emit_attn_b(x_cur, x_cur)
        if nsub >= 4:
            emit_ffn(1, x_cur, y_out)
        else:
            src = x_cur if nsub >= 1 else x_in
            sch.waits("sync", [last_x_store] if last_x_store else [])
            sch.dma("sync", y_out, src)

        sch.finish()

        with nc.Block() as block:
            @block.sync
            def _(e):
                for fn in sch.ops["sync"]:
                    fn(e)

            @block.gpsimd
            def _(e):
                for fn in sch.ops["gpsimd"]:
                    fn(e)

            @block.tensor
            def _(e):
                for fn in sch.ops["tensor"]:
                    fn(e)

            @block.vector
            def _(e):
                for fn in sch.ops["vector"]:
                    fn(e)

            @block.scalar
            def _(e):
                for fn in sch.ops["scalar"]:
                    fn(e)

    return nc


def kernel(**inputs):
    consts = host_constants()
    w = relayout_weights(inputs)
    nc = build_program(NSUB)
    base = {k: v for k, v in w.items()}
    for k in ("ident_bf", "ones1_bf", "ones1_f32", "ones128_bf", "DT",
              "cs_bc", "kv_bc", "tri1", "tri2"):
        base[k] = consts[k]
    x = inputs["L0_attn_a_x"]
    in_maps = [dict(base, x=np.ascontiguousarray(x[b])) for b in range(8)]
    res = run_bass_kernel_spmd(nc, in_maps, core_ids=list(range(8)))
    return np.stack([res.results[b]["y"] for b in range(8)], axis=0)


if __name__ == "__main__":
    rng = np.random.default_rng(0)
    fake = {
        "L0_attn_a_x": rng.standard_normal((8, S, D), dtype=np.float32),
    }
    print("smoke build only")
    build_program(1)
    print("built ok")
